# version 16
# baseline (speedup 1.0000x reference)
import numpy as np
import concourse.bass as bass
import concourse.mybir as mybir
from concourse.bass_utils import run_bass_kernel_spmd

F32 = mybir.dt.float32
BF16 = mybir.dt.bfloat16
I32 = mybir.dt.int32
AF = mybir.ActivationFunctionType
ALU = mybir.AluOpType
AXX = mybir.AxisListType.X

NB, T, D = 16, 2048, 1024
H, HD = 16, 64
E, II = 8, 2048
CORES = 8
SEQ = 2
TT = T // 128
DT = D // 128
KIT = II // 128
COLS = 2 * II // 128
PI = float(np.pi)
TWO_PI = float(2 * np.pi)
LOG10000 = float(np.log(10000.0))


class Builder:
    ENGINES = ("tensor", "vector", "scalar", "gpsimd", "sync")

    def __init__(self, nc, debug_taps=()):
        self.nc = nc
        self.debug_taps = set(debug_taps)
        self.ops = {e: [] for e in self.ENGINES}
        self.sems = {e: nc.alloc_semaphore(f"sem_{e}") for e in self.ENGINES}
        self.counts = {s: 0 for s in self.sems}
        self.dbg_outs = {}

    def op(self, eng, fn, deps=None):
        waits = self._merge(deps)
        self.counts[eng] += 1
        self.ops[eng].append((fn, waits, (self.sems[eng], 1)))
        return {eng: self.counts[eng]}

    def dma(self, eng, out, in_, deps=None, stream="misc", **kw):
        waits = self._merge(deps)
        if stream not in self.sems:
            self.sems[stream] = self.nc.alloc_semaphore(f"s_{stream}")
            self.counts[stream] = 0
        self.counts[stream] += 16
        def fn(e, out=out, in_=in_, kw=kw):
            return e.dma_start(out=out, in_=in_, **kw)
        self.ops[eng].append((fn, waits, (self.sems[stream], 16)))
        return {stream: self.counts[stream]}

    def _merge(self, deps):
        waits = {}
        for d in (deps or []):
            if not d:
                continue
            for k, v in d.items():
                if waits.get(k, -1) < v:
                    waits[k] = v
        return waits

    def tok_all(self, *names):
        names = names or list(self.counts)
        return {n: self.counts[n] for n in names if self.counts[n] > 0}

    def sem_of(self, name):
        return self.sems[name]

    def finalize(self):
        with self.nc.Block() as blk:
            for eng in self.ENGINES:
                entries = self.ops[eng]
                def body(engine, entries=entries):
                    waited = {}
                    for fn, waits, inc in entries:
                        for s, v in waits.items():
                            if waited.get(s, 0) < v:
                                engine.wait_ge(self.sem_of(s), v)
                                waited[s] = v
                        inst = fn(engine)
                        sem, n = inc
                        if inst is not None:
                            inst.then_inc(sem, n)
                        else:
                            engine.sem_inc(sem, n)
                getattr(blk, eng)(body)

    def tap(self, name, src_ap, shape, dep, dtype=F32):
        if name not in self.debug_taps or name in self.dbg_outs:
            return None
        t = self.nc.dram_tensor(f"dbg_{name}", list(shape), dtype, kind="ExternalOutput")
        self.dbg_outs[name] = t
        return self.dma("gpsimd", t.ap(), src_ap, deps=[dep], stream="tap")


def build_program(debug_taps=(), n_experts=E, do_ffn1=True, do_attn=True, do_ffn2=True):
    nc = bass.Bass("TRN2", target_bir_lowering=False)
    b = Builder(nc, debug_taps)

    x_in = nc.dram_tensor("L0_attn_x", [SEQ, T, D], F32, kind="ExternalInput").ap()
    w_qkv = nc.dram_tensor("L0_attn_w_qkv", [3 * D, D], F32, kind="ExternalInput").ap()
    w_out = nc.dram_tensor("L0_attn_w_out", [D, D], F32, kind="ExternalInput").ap()
    w_r = nc.dram_tensor("L0_ffn_router_weight", [E, D], F32, kind="ExternalInput").ap()
    b_r = nc.dram_tensor("L0_ffn_router_bias", [E], F32, kind="ExternalInput").ap()
    w_gu = nc.dram_tensor("L0_ffn_gate_up_proj", [E, 2 * II, D], F32, kind="ExternalInput").ap()
    b_gu = nc.dram_tensor("L0_ffn_gate_up_proj_bias", [E, 2 * II], F32, kind="ExternalInput").ap()
    w_d = nc.dram_tensor("L0_ffn_down_proj", [E, D, II], F32, kind="ExternalInput").ap()
    b_d = nc.dram_tensor("L0_ffn_down_proj_bias", [E, D], F32, kind="ExternalInput").ap()
    out = nc.dram_tensor("out", [SEQ, T, D], F32, kind="ExternalOutput").ap()

    gu_nat = nc.dram_tensor("gu_nat", [E, 2 * II, D], BF16).ap()
    wd_nat = nc.dram_tensor("wd_nat", [E, D, II], BF16).ap()
    qkv_nat = nc.dram_tensor("qkv_nat", [3 * D, D], BF16).ap()
    wout_nat = nc.dram_tensor("wout_nat", [D, D], BF16).ap()
    guT_blk = nc.dram_tensor("guT_blk", [E, COLS, 128, DT * 128], BF16).ap()
    wdT_blk = nc.dram_tensor("wdT_blk", [E, 2, 128, KIT, 512], BF16).ap()
    wqT_blk = nc.dram_tensor("wqT_blk", [DT, 128, D], BF16).ap()
    wkvT_blk = nc.dram_tensor("wkvT_blk", [DT, 128, 2 * D], BF16).ap()
    woutT_blk = nc.dram_tensor("woutT_blk", [DT, 128, D], BF16).ap()

    A = nc.alloc_sbuf_tensor
    x_sb = A("x_sb", [128, TT * D], F32).ap()
    h_fm = A("h_fm", [128, DT * T], BF16).ap()
    act_sb = A("act_sb", [128, 8 * T], BF16).ap()
    wstream = A("wstream", [128, 12288], BF16).ap()
    stage2 = A("stage2", [128, 4096], BF16, is_transpose_dest=True).ap()
    stages = None
    wgu_slot = [wstream[:, i * 1024:(i + 1) * 1024] for i in range(4)]
    wd_slot = [wstream[:, 4096 + i * 4096: 4096 + (i + 1) * 4096] for i in range(2)]
    wq_slot = [wstream[:, i * 1024:(i + 1) * 1024] for i in range(2)]
    wkv_slot = [wstream[:, 2048 + i * 2048: 2048 + (i + 1) * 2048] for i in range(2)]
    wout_slot = [wstream[:, 6144 + i * 1024: 6144 + (i + 1) * 1024] for i in range(2)]
    ident16 = A("ident16", [128, 128], BF16).ap()
    htm_scr = A("htm_scr", [128, D], BF16).ap()
    sg_scr = A("sg_scr", [128, 1024], F32).ap()
    u1_scr = A("u1_scr", [128, 1024], F32).ap()
    t_scr = A("t_scr", [128, 1024], F32).ap()
    sq_scr = t_scr
    cols_sb = A("cols_sb", [128, 64], F32).ap()
    iot_i = A("iot_i", [128, 132], I32).ap()
    iot_f = A("iot_f", [128, 132], F32).ap()
    cos_tm = A("cos_tm", [128, TT * 32], F32).ap()
    sin_tm = A("sin_tm", [128, TT * 32], F32).ap()
    logit_sb = A("logit_sb", [128, TT * 8], F32).ap()
    comb_sb = A("comb_sb", [128, TT * 8], F32).ap()
    comb16 = A("comb16", [128, 8], BF16).ap()
    comb_em = A("comb_em", [8, T], BF16).ap()
    bgcol = A("bgcol", [128, E * 16], F32).ap()
    bucol = A("bucol", [128, E * 16], F32).ap()
    brow_f = A("brow_f", [1, 1024], F32).ap()
    bd16 = A("bd16", [8, D], BF16).ap()
    wr16 = A("wr16", [8, D], BF16).ap()
    wrT = A("wrT", [128, DT * 8], BF16).ap()
    br16 = A("br16", [1, 8], BF16).ap()
    ones16 = A("ones16", [1, 128], BF16).ap()
    onesall16 = A("onesall16", [128, 64], BF16).ap()
    onesf32 = A("onesf32", [128, 64], F32).ap()
    kf_tt = A("kf_tt", [128, H * HD], BF16).ap()
    qf_tm_scr = A("qf_tm", [128, H * HD], BF16).ap()
    v_tt = A("v_tt", [128, H * 65 + 8], BF16).ap()
    kvks = A("kvks", [128, 8 * 66], BF16).ap()
    qsc_bf = A("qsc_bf", [128, 1024], BF16).ap()
    qf_tm2 = A("qf_tm2", [128, H * HD], BF16).ap()
    stages = [stage2[:, 0:2048], stage2[:, 2048:4096], kf_tt, v_tt[:, 0:1024]]
    qf_fm = act_sb
    out_all = h_fm

    P = nc.alloc_psum_tensor
    psA = P("psA", [128, 1024], F32).ap()
    psB = P("psB", [128, 1024], F32).ap()
    psC = P("psC", [128, 1024], F32).ap()
    psD = P("psD", [128, 1024], F32).ap()
    pairs = [psA, psB, psC, psD]

    t_i1 = b.op("gpsimd", lambda g: g.iota(iot_i[:, 0:1], pattern=[[0, 1]], channel_multiplier=1))
    t_i2 = b.op("gpsimd", lambda g: g.iota(iot_i[:, 1:129], pattern=[[1, 128]], channel_multiplier=0))
    t_if = b.op("vector", lambda v: v.tensor_copy(out=iot_f[:, 0:129], in_=iot_i[:, 0:129]),
                deps=[t_i1, t_i2])
    t_one_f = b.op("vector", lambda v: v.memset(iot_f[:, 130:131], 1.0))
    t_id = b.op("vector", lambda v: v.tensor_scalar(
        out=ident16, in0=iot_f[:, 1:129], scalar1=iot_f[:, 0:1], scalar2=None,
        op0=ALU.is_equal), deps=[t_if])
    t_ones = b.op("vector", lambda v: v.memset(ones16, 1.0))
    t_ones2 = b.op("vector", lambda v: v.memset(onesall16, 1.0))
    t_ones3 = b.op("vector", lambda v: v.memset(onesf32, 1.0))
    t_eps = b.op("vector", lambda v: v.memset(cols_sb[:, 46:47], 1e-5))
    t_b11 = b.op("vector", lambda v: v.memset(cols_sb[:, 47:48], float(1.702 * 7.0)))
    def emit_rope_tables():
        t_efr = b.op("scalar", lambda s: s.activation(
            out=cols_sb[:, 0:32], in_=iot_f[:, 1:33], func=AF.Exp, scale=-LOG10000 / 32.0),
            deps=[t_if])

        rope_last = [t_efr, t_if]
        for tt in range(TT):
            c0 = tt * 32
            tc = b.op("vector", lambda v, tt=tt: v.tensor_scalar(
                out=cols_sb[:, 33:34], in0=iot_f[:, 0:1], scalar1=float(128 * tt),
                scalar2=None, op0=ALU.add), deps=rope_last)
            ta = b.op("vector", lambda v, c0=c0: v.tensor_scalar(
                out=sin_tm[:, c0:c0 + 32], in0=cols_sb[:, 0:32], scalar1=cols_sb[:, 33:34],
                scalar2=None, op0=ALU.mult), deps=[tc])
            t1 = b.op("vector", lambda v, c0=c0: v.tensor_scalar(
                out=cos_tm[:, c0:c0 + 32], in0=sin_tm[:, c0:c0 + 32], scalar1=1.0 / TWO_PI,
                scalar2=None, op0=ALU.mult), deps=[ta])
            t2 = b.op("vector", lambda v, c0=c0: v.tensor_copy(
                out=iot_i[:, 100:132], in_=cos_tm[:, c0:c0 + 32]), deps=[t1])
            t3 = b.op("vector", lambda v, c0=c0: v.tensor_copy(
                out=cos_tm[:, c0:c0 + 32], in_=iot_i[:, 100:132]), deps=[t2])
            t4 = b.op("vector", lambda v, c0=c0: v.scalar_tensor_tensor(
                out=sin_tm[:, c0:c0 + 32], in0=cos_tm[:, c0:c0 + 32], scalar=-TWO_PI,
                in1=sin_tm[:, c0:c0 + 32], op0=ALU.mult, op1=ALU.add), deps=[t3])
            t5 = b.op("vector", lambda v, c0=c0: v.tensor_scalar(
                out=cos_tm[:, c0:c0 + 32], in0=sin_tm[:, c0:c0 + 32], scalar1=PI / 2,
                scalar2=None, op0=ALU.add), deps=[t4])
            t6 = b.op("vector", lambda v, c0=c0: v.tensor_scalar(
                out=iot_f[:, 96:128], in0=cos_tm[:, c0:c0 + 32], scalar1=PI,
                scalar2=TWO_PI, op0=ALU.is_gt, op1=ALU.mult), deps=[t5])
            t7 = b.op("vector", lambda v, c0=c0: v.tensor_tensor(
                out=cos_tm[:, c0:c0 + 32], in0=cos_tm[:, c0:c0 + 32],
                in1=iot_f[:, 96:128], op=ALU.subtract), deps=[t6])
            t8 = b.op("scalar", lambda s, c0=c0: s.activation(
                out=sin_tm[:, c0:c0 + 32], in_=sin_tm[:, c0:c0 + 32], func=AF.Sin), deps=[t4, t5])
            t9 = b.op("scalar", lambda s, c0=c0: s.activation(
                out=cos_tm[:, c0:c0 + 32], in_=cos_tm[:, c0:c0 + 32], func=AF.Sin), deps=[t7])
            rope_last = [t8, t9]
        rope_done = b._merge(rope_last)
        return rope_done


    t_wr = b.dma("gpsimd", wr16, w_r, stream="wr")
    t_brr = b.dma("gpsimd", br16, b_r.unsqueeze(0), stream="br")
    t_bd = b.dma("gpsimd", bd16, b_d, stream="bd")

    wrT_done = []
    for dt in range(DT):
        tp = b.op("tensor", lambda t, dt=dt: t.transpose(
            psA.bitcast(BF16)[:, 0:8], wr16[:, dt * 128:(dt + 1) * 128], ident16[0:8, 0:8]),
            deps=[t_wr, t_id] + wrT_done[-1:])
        te = b.op("vector", lambda v, dt=dt: v.tensor_copy(
            out=wrT[:, dt * 8:(dt + 1) * 8], in_=psA.bitcast(BF16)[:, 0:8]), deps=[tp])
        wrT_done.append(te)
    wrT_done = b._merge(wrT_done)

    b_iter = [(e, half, p8) for e in range(n_experts) for half in range(2)
              for p8 in range(2)]
    tl_toks = {}
    mm_toks_b = {}
    ev_toks_b = {}
    for i, (e, half, p8) in enumerate(b_iter):
        par = i % 2
        off = half * 2048 + p8 * 1024
        tla = b.dma("sync", brow_f[0:1, 0:512],
                    b_gu[e].unsqueeze(0)[0:1, off:off + 512],
                    deps=[mm_toks_b.get(i - 1), t_one_f, t_if], stream="brow0")
        tlb = b.dma("sync", brow_f[0:1, 512:1024],
                    b_gu[e].unsqueeze(0)[0:1, off + 512:off + 1024],
                    deps=[mm_toks_b.get(i - 1)], stream="brow0")
        tl_toks[i] = b._merge([tla, tlb])
        def mmf(t, par=par):
            last = None
            for k in range(8):
                src = brow_f[0:1, k * 128:(k + 1) * 128]
                last = t.matmul(psB[:, par * 512 + k: par * 512 + k + 1], src,
                                iot_f[0:1, 130:131], start=(k == 0), stop=(k == 7))
            return last
        mm_toks_b[i] = b.op("tensor", mmf, deps=[tl_toks[i], ev_toks_b.get(i - 2)])
        dst = bgcol if half == 0 else bucol
        def evf(v, e=e, half=half, p8=p8, dst=dst, par=par):
            c0 = e * 16 + p8 * 8
            if half == 0:
                return v.tensor_scalar(out=dst[:, c0:c0 + 8],
                                       in0=psB[:, par * 512: par * 512 + 8],
                                       scalar1=-1.0, scalar2=7.0,
                                       op0=ALU.mult, op1=ALU.add)
            return v.tensor_scalar(out=dst[:, c0:c0 + 8],
                                   in0=psB[:, par * 512: par * 512 + 8],
                                   scalar1=7.0, scalar2=None, op0=ALU.add)
        ev_toks_b[i] = b.op("vector", evf, deps=[mm_toks_b[i]])
    bias_done = b._merge(list(ev_toks_b.values()))
    b.tap("bgcol", bgcol, [128, E * 16], bias_done)
    b.tap("bucol", bucol, [128, E * 16], bias_done)

    early_x0 = b.dma("gpsimd", x_sb.rearrange("p (tt d) -> p tt d", d=D),
                     x_in[0].rearrange("(tt p) d -> p tt d", p=128), stream="xin")

    cast_toks = {}
    stage_free = [None, None, None, None]
    stage_ctr = [0, 0]
    blk_toks = {}

    def transp_unit(ck, src, width, dst, dst_src_re=None, chain=0):
        par = chain * 2 + stage_ctr[chain] % 2
        stage_ctr[chain] += 1
        stg = stages[par]
        tt1 = b.dma("scalar", stg[:, 0:width], src,
                    deps=[cast_toks[ck], stage_free[par]], transpose=True,
                    stream=f"staget{par}")
        if dst_src_re is None:
            src2, dst2 = stg[:, 0:width], dst
            toks = [b.dma("gpsimd", dst2, src2, deps=[tt1], stream=f"stages{par}")]
        else:
            sre, dre, kw = dst_src_re
            src2 = stg[:, 0:width].rearrange(sre, **kw)
            dst2 = dst.rearrange(dre)
            nb = src2.shape[1]
            step = max(1, 512 // 128)
            toks = []
            for i0 in range(0, nb, step):
                i1 = min(nb, i0 + step)
                toks.append(b.dma("gpsimd", dst2[:, i0:i1], src2[:, i0:i1],
                                  deps=[tt1], stream=f"stages{par}"))
        tok = b._merge(toks)
        stage_free[par] = tok
        return tok

    def emit_pro_cast(e):
        cast_toks[("gu", e)] = b.dma("gpsimd", gu_nat[e], w_gu[e],
                                     deps=[cast_toks.get(("gu", e - 2))],
                                     stream=f"castgu{e % 2}")
        cast_toks[("wd", e)] = b.dma("gpsimd", wd_nat[e], w_d[e],
                                     deps=[cast_toks.get(("wd", e - 2))],
                                     stream=f"castwd{e % 2}")

    pro_queue = []

    def _gu_unit(e, dt, jh):
        tok = transp_unit(
            ("gu", e),
            gu_nat[e][jh * 2048:(jh + 1) * 2048, dt * 128:(dt + 1) * 128], 2048,
            guT_blk[e, jh * 16:(jh + 1) * 16, :, dt * 128:(dt + 1) * 128],
            ("p (col jj) -> p col jj", "col p jj -> p col jj", {"jj": 128}))
        key = ("gu", e, dt)
        blk_toks[key] = b._merge([blk_toks.get(key), tok])

    def _wd_unit(e, it):
        blk_toks[("wd", e, it)] = transp_unit(
            ("wd", e),
            wd_nat[e][:, it * 128:(it + 1) * 128], 1024,
            wdT_blk[e, :, :, it, :],
            ("p (dc d) -> p dc d", "dc p d -> p dc d", {"d": 512}), chain=1)

    def queue_pro_expert(e):
        for dt in range(DT):
            for jh in range(2):
                pro_queue.append(lambda e=e, dt=dt, jh=jh: _gu_unit(e, dt, jh))
        for it in range(KIT):
            pro_queue.append(lambda e=e, it=it: _wd_unit(e, it))

    def emit_pro_expert(e):
        queue_pro_expert(e)
        while pro_queue:
            pro_queue.pop(0)()

    def pro_step(n=3):
        for _ in range(n):
            if pro_queue:
                pro_queue.pop(0)()

    def emit_pro_attn_cast():
        cast_toks["qkv"] = b.dma("gpsimd", qkv_nat, w_qkv, stream="castq")
        cast_toks["wout"] = b.dma("gpsimd", wout_nat, w_out, stream="castw")

    def emit_pro_attn():
        def u1(dt):
            blk_toks[("wq", dt)] = transp_unit(
                "qkv", qkv_nat[0:D, dt * 128:(dt + 1) * 128], 1024, wqT_blk[dt], chain=1)
        def u2(dt):
            blk_toks[("wkv", dt)] = transp_unit(
                "qkv", qkv_nat[D:3 * D, dt * 128:(dt + 1) * 128], 2048, wkvT_blk[dt])
        def u3(dt):
            blk_toks[("wout", dt)] = transp_unit(
                "wout", wout_nat[:, dt * 128:(dt + 1) * 128], 1024, woutT_blk[dt], chain=1)
        for dt in range(DT):
            pro_queue.append(lambda dt=dt: u1(dt))
            pro_queue.append(lambda dt=dt: u2(dt))
            pro_queue.append(lambda dt=dt: u3(dt))

    prologue_compute = b.tok_all("tensor", "vector", "scalar", "gpsimd")

    def emit_ln(deps):
        toks = []
        serial = []
        prev_tp = []
        for tt in range(TT):
            xt = x_sb[:, tt * D:(tt + 1) * D]
            t_s = b.op("vector", lambda v, xt=xt: v.reduce_sum(
                out=cols_sb[:, 40:41], in_=xt, axis=AXX), deps=deps + serial[-1:])
            t_q = b.op("scalar", lambda s, xt=xt: s.activation(
                out=sq_scr, in_=xt, func=AF.Square, accum_out=cols_sb[:, 41:42]),
                deps=deps + serial[-1:])
            t_m = b.op("vector", lambda v: v.tensor_scalar(
                out=cols_sb[:, 42:43], in0=cols_sb[:, 40:41], scalar1=1.0 / D,
                scalar2=None, op0=ALU.mult), deps=[t_s])
            t_v = b.op("vector", lambda v: v.scalar_tensor_tensor(
                out=cols_sb[:, 43:44], in0=cols_sb[:, 42:43], scalar=-1.0,
                in1=cols_sb[:, 42:43], op0=ALU.mult, op1=ALU.mult), deps=[t_m])
            t_v2 = b.op("vector", lambda v: v.scalar_tensor_tensor(
                out=cols_sb[:, 43:44], in0=cols_sb[:, 41:42], scalar=1.0 / D,
                in1=cols_sb[:, 43:44], op0=ALU.mult, op1=ALU.add), deps=[t_v, t_q])
            t_sd = b.op("scalar", lambda s: s.activation(
                out=cols_sb[:, 44:45], in_=cols_sb[:, 43:44], func=AF.Sqrt,
                bias=cols_sb[:, 46:47]), deps=[t_v2, t_eps])
            t_r = b.op("vector", lambda v: v.reciprocal(
                out=cols_sb[:, 45:46], in_=cols_sb[:, 44:45]), deps=[t_sd])
            t_h = b.op("vector", lambda v, xt=xt: v.tensor_scalar(
                out=htm_scr, in0=xt, scalar1=cols_sb[:, 42:43], scalar2=cols_sb[:, 45:46],
                op0=ALU.subtract, op1=ALU.mult), deps=[t_r])
            serial.append(t_h)
            for dt in range(DT):
                ps = pairs[dt % 2].bitcast(BF16)[:, 0:128]
                t_tp = b.op("tensor", lambda t, dt=dt, ps=ps: t.transpose(
                    ps, htm_scr[:, dt * 128:(dt + 1) * 128], ident16),
                    deps=[t_h, t_id] + prev_tp[-2:])
                t_ev = b.op("vector", lambda v, dt=dt, tt=tt, ps=ps: v.tensor_copy(
                    out=h_fm[:, dt * T + tt * 128: dt * T + tt * 128 + 128], in_=ps),
                    deps=[t_tp])
                prev_tp.append(t_ev)
                toks.append(t_ev)
        return b._merge(toks)

    def emit_router(deps):
        toks = []
        prev = []
        for tt in range(TT):
            def mmf(t, tt=tt):
                for dt in range(DT):
                    t.matmul(psC[:, 0:8],
                             h_fm[:, dt * T + tt * 128: dt * T + tt * 128 + 128],
                             wrT[:, dt * 8:(dt + 1) * 8],
                             start=(dt == 0), stop=False)
                return t.matmul(psC[:, 0:8], ones16, br16, start=False, stop=True)
            t_mm = b.op("tensor", mmf, deps=deps + [wrT_done, t_brr, t_ones] + prev[-1:])
            t_cp = b.op("vector", lambda v, tt=tt: v.tensor_copy(
                out=logit_sb[:, tt * 8:(tt + 1) * 8], in_=psC[:, 0:8]), deps=[t_mm])
            prev.append(t_cp)
        ltok = b._merge(prev)
        prev = []
        for tt in range(TT):
            lg = logit_sb[:, tt * 8:(tt + 1) * 8]
            cb = comb_sb[:, tt * 8:(tt + 1) * 8]
            t_m8 = b.op("vector", lambda v, lg=lg: v.max(out=cols_sb[:, 48:56], in_=lg),
                        deps=[ltok] + prev[-1:])
            t_nm = b.op("vector", lambda v: v.tensor_scalar(
                out=cols_sb[:, 56:57], in0=cols_sb[:, 48:49], scalar1=-1.0, scalar2=None,
                op0=ALU.mult), deps=[t_m8])
            t_e = b.op("scalar", lambda s, lg=lg, cb=cb: s.activation(
                out=cb, in_=lg, func=AF.Exp, bias=cols_sb[:, 56:57]), deps=[t_nm])
            t_msk = b.op("vector", lambda v, lg=lg: v.tensor_scalar(
                out=t_scr[:, 32:40], in0=lg, scalar1=cols_sb[:, 51:52], scalar2=None,
                op0=ALU.is_ge), deps=[t_m8])
            t_em = b.op("vector", lambda v, cb=cb: v.tensor_tensor(
                out=cb, in0=cb, in1=t_scr[:, 32:40], op=ALU.mult), deps=[t_msk, t_e])
            t_sum = b.op("vector", lambda v, cb=cb: v.reduce_sum(
                out=cols_sb[:, 57:58], in_=cb, axis=AXX), deps=[t_em])
            t_rc = b.op("vector", lambda v: v.reciprocal(
                out=cols_sb[:, 58:59], in_=cols_sb[:, 57:58]), deps=[t_sum])
            t_cb = b.op("vector", lambda v, cb=cb: v.tensor_scalar(
                out=cb, in0=cb, scalar1=cols_sb[:, 58:59], scalar2=0.5,
                op0=ALU.mult, op1=ALU.mult), deps=[t_rc])
            t_c16 = b.op("vector", lambda v, cb=cb: v.tensor_copy(out=comb16, in_=cb),
                         deps=[t_cb])
            t_tp = b.op("tensor", lambda t: t.transpose(
                psD.bitcast(BF16)[0:8, 0:128], comb16, ident16),
                deps=[t_c16, t_id] + prev[-1:])
            t_em2 = b.op("vector", lambda v, tt=tt: v.tensor_copy(
                out=comb_em[:, tt * 128:(tt + 1) * 128],
                in_=psD.bitcast(BF16)[0:8, 0:128]), deps=[t_tp])
            prev.append(b._merge([t_cb, t_em2]))
            toks.append(prev[-1])
        return b._merge(toks)

    def gu_blk_done(e):
        return b._merge([blk_toks[("gu", e, dt)] for dt in range(DT)])

    def wd_blk_done(e):
        return b._merge([blk_toks[("wd", e, it)] for it in range(KIT)])

    def emit_ffn(deps, arena_free, prefetch_cb=None):
        x_toks = []
        slot_free = {("gu", 0): arena_free, ("gu", 1): arena_free,
                     ("gu", 2): arena_free, ("gu", 3): arena_free,
                     ("wd", 0): arena_free, ("wd", 1): arena_free}
        gu_pair_ctr = [0]
        pg_free = {0: None, 1: None}
        dn_free = {0: None, 1: None}
        scr_free = [None]
        u_free = [None]
        act_ready = {}
        act_pe_done = [None]

        for e in range(n_experts):
            if prefetch_cb:
                prefetch_cb(e, -1)
            for half in range(2):
                ap_done = act_pe_done[0]
                wd_toks = {}
                for dchunk in range(2):
                    sl = dchunk
                    wd_toks[dchunk] = b.dma(
                        "sync", wd_slot[sl].rearrange("p (k d) -> p k d", d=512),
                        wdT_blk[e, dchunk, :, half * 8:(half + 1) * 8, :],
                        deps=[wd_blk_done(e), slot_free[("wd", sl)]],
                        stream=f"wd{sl}")
                for p_idx in range(8):
                    if prefetch_cb:
                        prefetch_cb(e, half * 8 + p_idx)
                    gcol = half * 8 + p_idx
                    ucol = 16 + half * 8 + p_idx
                    par = gu_pair_ctr[0] % 2
                    gsl, usl = par, 2 + par
                    tg = b.dma("sync", wgu_slot[gsl], guT_blk[e, gcol],
                               deps=deps + [gu_blk_done(e), slot_free[("gu", gsl)]],
                               stream=f"wgu{gsl}")
                    tu = b.dma("sync", wgu_slot[usl], guT_blk[e, ucol],
                               deps=deps + [gu_blk_done(e), slot_free[("gu", usl)]],
                               stream=f"wgu{usl}")
                    gu_pair_ctr[0] += 1
                    pe_toks = {}
                    for c in range(2):
                        pg, pu = pairs[2 * c], pairs[2 * c + 1]
                        def mmf(t, gsl=gsl, usl=usl, c=c, pg=pg, pu=pu):
                            last = None
                            for which, sl, ps in (("g", gsl, pg), ("u", usl, pu)):
                                for kt in range(DT):
                                    for n in range(2):
                                        rhs = h_fm[:, kt * T + c * 1024 + n * 512:
                                                   kt * T + c * 1024 + (n + 1) * 512]
                                        last = t.matmul(ps[:, n * 512:(n + 1) * 512],
                                                        wgu_slot[sl][:, kt * 128:(kt + 1) * 128],
                                                        rhs, start=(kt == 0), stop=(kt == DT - 1))
                            return last
                        pe_toks[c] = b.op("tensor", mmf,
                                          deps=deps + [tg, tu, pg_free[c], dn_free[c],
                                                       act_pe_done[0]])
                    dbg_tok = None
                    if e == 0 and half == 0 and p_idx == 0 and "gate_ps" in b.debug_taps:
                        dbg_tok = b.op("vector", lambda v: v.tensor_copy(
                            out=htm_scr, in_=pairs[0]), deps=[pe_toks[0]])
                        b.tap("gate_ps", htm_scr, [128, 1024], dbg_tok, dtype=BF16)
                        dbg_tok = b.tok_all("vector", "tap")
                        pe_toks = {0: b._merge([pe_toks[0], dbg_tok]), 1: pe_toks[1]}
                    for c in range(2):
                        pg, pu = pairs[2 * c], pairs[2 * c + 1]
                        bgc = bgcol[:, e * 16 + gcol: e * 16 + gcol + 1]
                        buc = bucol[:, e * 16 + gcol: e * 16 + gcol + 1]
                        a1 = b.op("scalar", lambda s, pg=pg, bgc=bgc: s.activation(
                            out=t_scr, in_=pg, func=AF.Relu, bias=bgc, scale=-1.0),
                            deps=[pe_toks[c], bias_done, scr_free[0]])
                        a2 = b.op("scalar", lambda s: s.activation(
                            out=sg_scr, in_=t_scr, func=AF.Sigmoid, bias=cols_sb[:, 47:48],
                            scale=-1.702), deps=[a1, t_b11])
                        a3 = b.op("scalar", lambda s, pu=pu, buc=buc: s.activation(
                            out=u1_scr, in_=pu, func=AF.Relu, bias=buc), deps=[a2, u_free[0]])
                        pg_free[c] = b._merge([a3, dbg_tok]) if dbg_tok else a3
                        d1 = b.op("vector", lambda v: v.tensor_scalar(
                            out=t_scr, in0=t_scr, scalar1=-1.0, scalar2=7.0,
                            op0=ALU.mult, op1=ALU.add), deps=[a1, a2])
                        d2 = b.op("vector", lambda v: v.tensor_tensor(
                            out=t_scr, in0=t_scr, in1=sg_scr, op=ALU.mult), deps=[d1, a2])
                        d3 = b.op("vector", lambda v: v.tensor_scalar(
                            out=u1_scr, in0=u1_scr, scalar1=14.0, scalar2=-6.0,
                            op0=ALU.min, op1=ALU.add), deps=[a3])
                        dst = act_sb[:, p_idx * T + c * 1024: p_idx * T + (c + 1) * 1024]
                        d4 = b.op("vector", lambda v, dst=dst: v.tensor_tensor(
                            out=dst, in0=t_scr, in1=u1_scr, op=ALU.mult),
                            deps=[d2, d3, ap_done])
                        u_free[0] = d4
                        scr_free[0] = d4
                        act_ready[(p_idx, c)] = d4
                    slot_free[("gu", gsl)] = pe_toks[1]
                    slot_free[("gu", usl)] = pe_toks[1]
                act_all = b._merge([act_ready[k] for k in act_ready])
                if e == 0 and half == 0:
                    b.tap("act_e0h0", act_sb, [128, 8 * T], act_all, dtype=BF16)
                dn_pe = []
                for dchunk in range(2):
                    for ttb in range(4):
                        bp = ttb % 2
                        def mmf(t, e=e, half=half, dchunk=dchunk, ttb=ttb, bp=bp):
                            last = None
                            for kit8 in range(8):
                                for tti in range(4):
                                    tt = ttb * 4 + tti
                                    lhs = act_sb[:, kit8 * T + tt * 128:
                                                 kit8 * T + tt * 128 + 128]
                                    ps = pairs[2 * bp + tti // 2][:, (tti % 2) * 512:
                                                                  (tti % 2 + 1) * 512]
                                    first = (kit8 == 0)
                                    if first and e == 0 and half == 0:
                                        t.matmul(ps, comb_em[:, tt * 128:(tt + 1) * 128],
                                                 bd16[0:8, dchunk * 512:(dchunk + 1) * 512],
                                                 start=True, stop=False)
                                        first = False
                                    last = t.matmul(
                                        ps, lhs,
                                        wd_slot[dchunk][:, kit8 * 512:(kit8 + 1) * 512],
                                        start=first, stop=(kit8 == 7))
                            return last
                        tmm = b.op("tensor", mmf,
                                   deps=[act_all, wd_toks[dchunk], dn_free[bp], pg_free[0],
                                         pg_free[1]])
                        stt_toks = []
                        for tti in range(4):
                            tt = ttb * 4 + tti
                            ps = pairs[2 * bp + tti // 2][:, (tti % 2) * 512:(tti % 2 + 1) * 512]
                            xsl = x_sb[:, tt * D + dchunk * 512: tt * D + dchunk * 512 + 512]
                            ccol = comb_sb[:, tt * 8 + e: tt * 8 + e + 1]
                            tst = b.op("vector", lambda v, ps=ps, xsl=xsl, ccol=ccol:
                                       v.scalar_tensor_tensor(
                                           out=xsl, in0=ps, scalar=ccol, in1=xsl,
                                           op0=ALU.mult, op1=ALU.add), deps=[tmm])
                            stt_toks.append(tst)
                        dn_free[bp] = b._merge(stt_toks)
                        dn_pe.append(tmm)
                        x_toks.extend(stt_toks)
                act_pe_done[0] = b._merge(dn_pe)
                slot_free[("wd", 0)] = dn_pe[3]
                slot_free[("wd", 1)] = dn_pe[7]
        gate = b._merge([act_pe_done[0], pg_free[0], pg_free[1], dn_free[0], dn_free[1],
                         u_free[0], scr_free[0],
                         slot_free[("gu", 0)], slot_free[("gu", 1)],
                         slot_free[("gu", 2)], slot_free[("gu", 3)]])
        return b._merge(x_toks), gate

    at_wq = [wstream[:, i * 1024:(i + 1) * 1024] for i in range(2)]
    at_wkv = [wstream[:, 2048 + i * 2048: 2048 + (i + 1) * 2048] for i in range(2)]
    at_wo = [wstream[:, 6144 + i * 1024: 6144 + (i + 1) * 1024] for i in range(2)]

    def emit_attn(deps, arena_free):
        deps = deps + [stage_free[2], stage_free[3]]
        t_vones = b.op("vector", lambda v: v.memset(
            v_tt[:, 0:H * 65].rearrange("p (h w) -> p h w", w=65)[:, :, 64:65], 1.0),
            deps=deps)
        cos, sin = cos_tm, sin_tm
        slotf = {}
        for nm in ("wq", "wkv", "wo"):
            slotf[(nm, 0)] = arena_free
            slotf[(nm, 1)] = arena_free
        kv_mm_tok = [None]
        tp_last = [None]
        v1cp_last = [None]
        scr_tok = [b._merge(deps)]
        krope_ps = {0: None, 1: None}
        vcp0_last = [None]

        def kv_ps(h):
            pr = h // 2
            base = (h % 2) * 64
            bank = pr // 4
            off = (pr % 4) * 65
            return psC[base:base + 64, bank * 512 + off: bank * 512 + off + 65]

        pend = {}
        qf_tms = [qf_tm_scr, qf_tm2]

        def emit_tail(tt, cur_vq=None):
            st = pend.pop(tt)
            def kvmm(t, tt=tt):
                last = None
                for h in range(H):
                    tp = (0, (h % 2) * 64)
                    first = (tt == 0) and (h in (0, 1, 8, 9))
                    last = t.matmul(kv_ps(h), kf_tt[:, h * 64:(h + 1) * 64],
                                    v_tt[:, h * 65: h * 65 + 65],
                                    start=first, stop=(tt == TT - 1),
                                    tile_position=tp)
                return last
            kv_mm_tok[0] = b.op("tensor", kvmm, deps=[st["k_done"], st["vc"]])
            qtm = qf_tms[tt % 2]
            tps = []
            for dt in range(DT):
                pst = psA.bitcast(BF16)[:, 0:128] if dt % 2 == 0 else \
                    psA.bitcast(BF16)[:, 128:256]
                ttp = b.op("tensor", lambda t, pst=pst, dt=dt, qtm=qtm: t.transpose(
                    pst, qtm[:, dt * 128:(dt + 1) * 128], ident16),
                    deps=[st["q_elu"], t_id, cur_vq or st["vq"]] + tps[-2:])
                tev = b.op("vector", lambda v, pst=pst, dt=dt, tt=tt: v.tensor_copy(
                    out=qf_fm[:, dt * T + tt * 128: dt * T + tt * 128 + 128], in_=pst),
                    deps=[ttp])
                tps.append(tev)
            tp_last[0] = b._merge(tps)

        for tt in range(TT):
            p1_toks = []
            for dt in range(DT):
                sl = dt % 2
                twq = b.dma("sync", at_wq[sl], wqT_blk[dt],
                            deps=[blk_toks[("wq", dt)], slotf[("wq", sl)]],
                            stream=f"wgu{sl}")
                twk = b.dma("sync", at_wkv[sl], wkvT_blk[dt],
                            deps=[blk_toks[("wkv", dt)], slotf[("wkv", sl)]],
                            stream=f"wgu{2 + sl}")
                def p1(t, tt=tt, dt=dt, sl=sl):
                    hsl = h_fm[:, dt * T + tt * 128: dt * T + tt * 128 + 128]
                    st_, sp = (dt == 0), (dt == DT - 1)
                    t.matmul(psA[:, 0:512], hsl, at_wq[sl][:, 0:512], start=st_, stop=sp)
                    t.matmul(psA[:, 512:1024], hsl, at_wq[sl][:, 512:1024], start=st_, stop=sp)
                    t.matmul(psB[:, 0:512], hsl, at_wkv[sl][:, 0:512], start=st_, stop=sp)
                    t.matmul(psB[:, 512:1024], hsl, at_wkv[sl][:, 512:1024], start=st_, stop=sp)
                    t.matmul(psD[:, 0:512], hsl, at_wkv[sl][:, 1024:1536], start=st_, stop=sp)
                    return t.matmul(psD[:, 512:1024], hsl, at_wkv[sl][:, 1536:2048],
                                    start=st_, stop=sp)
                dep1 = [twq, twk] + ([p1_toks[-2]] if len(p1_toks) >= 2 else
                                     [scr_tok[0], tp_last[0], v1cp_last[0],
                                      krope_ps[0], vcp0_last[0]])
                tok = b.op("tensor", p1, deps=dep1)
                p1_toks.append(tok)
                slotf[("wq", sl)] = tok
                slotf[("wkv", sl)] = tok
            p1_done = p1_toks[-1]
            vq = b.op("scalar", lambda s_: s_.copy(out=qsc_bf, in_=psA),
                      deps=[p1_done, scr_tok[0]])
            if tt > 0:
                emit_tail(tt - 1, cur_vq=vq)
            vc0 = b.op("scalar", lambda s_: s_.copy(
                out=v_tt[:, 0:H * 65].rearrange("p (h w) -> p h w", w=65)[:, 0:8, 0:64],
                in_=psD[:, 0:512].rearrange("p (h w) -> p h w", w=64)),
                deps=[p1_done, kv_mm_tok[0]])
            vcp0_last[0] = vc0
            vc1 = b.op("scalar", lambda s_: s_.copy(
                out=v_tt[:, 0:H * 65].rearrange("p (h w) -> p h w", w=65)[:, 8:16, 0:64],
                in_=psD[:, 512:1024].rearrange("p (h w) -> p h w", w=64)),
                deps=[p1_done, kv_mm_tok[0]])
            v1cp_last[0] = vc1
            pq = qsc_bf.rearrange("p (h w) -> p h w", w=64)
            cce = cos[:, tt * 32:(tt + 1) * 32].unsqueeze(1).to_broadcast([128, H, 32])
            sse = sin[:, tt * 32:(tt + 1) * 32].unsqueeze(1).to_broadcast([128, H, 32])
            tv = t_scr.rearrange("p (h w) -> p h w", w=32)
            sv = sg_scr.rearrange("p (h w) -> p h w", w=32)
            r1 = b.op("vector", lambda v, pq=pq, cce=cce: v.scalar_tensor_tensor(
                out=tv[:, 0:16], in0=pq[:, :, 0:32], scalar=0.125, in1=cce,
                op0=ALU.mult, op1=ALU.mult), deps=[vq])
            r2 = b.op("vector", lambda v, pq=pq, sse=sse: v.scalar_tensor_tensor(
                out=tv[:, 16:32], in0=pq[:, :, 32:64], scalar=0.125, in1=sse,
                op0=ALU.mult, op1=ALU.mult), deps=[vq])
            r3 = b.op("vector", lambda v, pq=pq, sse=sse: v.scalar_tensor_tensor(
                out=sv[:, 0:16], in0=pq[:, :, 0:32], scalar=0.125, in1=sse,
                op0=ALU.mult, op1=ALU.mult), deps=[vq])
            r4 = b.op("vector", lambda v, pq=pq, cce=cce: v.scalar_tensor_tensor(
                out=sv[:, 16:32], in0=pq[:, :, 32:64], scalar=0.125, in1=cce,
                op0=ALU.mult, op1=ALU.mult), deps=[vq])
            q_rope = b._merge([r1, r2, r3, r4])
            r5 = b.op("vector", lambda v: v.tensor_sub(
                out=t_scr[:, 0:512], in0=t_scr[:, 0:512], in1=t_scr[:, 512:1024]),
                deps=[r1, r2])
            r6 = b.op("vector", lambda v: v.tensor_add(
                out=sg_scr[:, 0:512], in0=sg_scr[:, 0:512], in1=sg_scr[:, 512:1024]),
                deps=[r3, r4])
            qtm = qf_tms[tt % 2].rearrange("p (h w) -> p h w", w=64)
            el_toks = []
            prev_e2 = None
            for src, dst_sl in ((t_scr, 0), (sg_scr, 32)):
                e1 = b.op("vector", lambda v, src=src: v.tensor_scalar(
                    out=u1_scr[:, 0:512], in0=src[:, 0:512], scalar1=0.0, scalar2=None,
                    op0=ALU.min), deps=[r5, r6, prev_e2])
                e2 = b.op("scalar", lambda s_: s_.activation(
                    out=u1_scr[:, 512:1024], in_=u1_scr[:, 0:512], func=AF.Exp), deps=[e1])
                prev_e2 = e2
                e4 = b.op("vector", lambda v, src=src, dst_sl=dst_sl, qtm=qtm:
                          v.scalar_tensor_tensor(
                              out=qtm[:, :, dst_sl:dst_sl + 32],
                              in0=src.rearrange("p (h w) -> p h w", w=32)[:, 0:16],
                              scalar=0.0,
                              in1=u1_scr[:, 512:1024].rearrange("p (h w) -> p h w", w=32)[:, 0:16],
                              op0=ALU.max, op1=ALU.add), deps=[e2])
                el_toks.append(e4)
            q_elu = b._merge(el_toks)
            vk = b.op("scalar", lambda s_: s_.copy(out=qsc_bf, in_=psB),
                      deps=[q_rope, p1_done, prev_e2])
            krope_ps[0] = vk
            krope_ps[1] = vk
            ke_all = []
            prev_e2k = prev_e2
            for kh in range(2):
                pk = qsc_bf[:, kh * 512:(kh + 1) * 512].rearrange("p (h w) -> p h w", w=64)
                cc8 = cos[:, tt * 32:(tt + 1) * 32].unsqueeze(1).to_broadcast([128, 8, 32])
                ss8 = sin[:, tt * 32:(tt + 1) * 32].unsqueeze(1).to_broadcast([128, 8, 32])
                tv8 = t_scr.rearrange("p (h w) -> p h w", w=32)
                sv8 = sg_scr.rearrange("p (h w) -> p h w", w=32)
                k1 = b.op("vector", lambda v, pk=pk, cc8=cc8: v.tensor_tensor(
                    out=tv8[:, 0:8], in0=pk[:, :, 0:32], in1=cc8, op=ALU.mult),
                    deps=[vk, q_elu])
                k2 = b.op("vector", lambda v, pk=pk, ss8=ss8: v.tensor_tensor(
                    out=tv8[:, 8:16], in0=pk[:, :, 32:64], in1=ss8, op=ALU.mult),
                    deps=[vk])
                k3 = b.op("vector", lambda v, pk=pk, ss8=ss8: v.tensor_tensor(
                    out=sv8[:, 0:8], in0=pk[:, :, 0:32], in1=ss8, op=ALU.mult),
                    deps=[vk])
                k4 = b.op("vector", lambda v, pk=pk, cc8=cc8: v.tensor_tensor(
                    out=sv8[:, 8:16], in0=pk[:, :, 32:64], in1=cc8, op=ALU.mult),
                    deps=[vk])
                k5 = b.op("vector", lambda v: v.tensor_sub(
                    out=t_scr[:, 0:256], in0=t_scr[:, 0:256], in1=t_scr[:, 256:512]),
                    deps=[k1, k2])
                k6 = b.op("vector", lambda v: v.tensor_add(
                    out=sg_scr[:, 0:256], in0=sg_scr[:, 0:256], in1=sg_scr[:, 256:512]),
                    deps=[k3, k4])
                ktm = kf_tt[:, kh * 512:(kh + 1) * 512].rearrange("p (h w) -> p h w", w=64)
                for src, dst_sl in ((t_scr, 0), (sg_scr, 32)):
                    e1 = b.op("vector", lambda v, src=src: v.tensor_scalar(
                        out=u1_scr[:, 0:256], in0=src[:, 0:256], scalar1=0.0, scalar2=None,
                        op0=ALU.min), deps=[k5, k6, prev_e2k, kv_mm_tok[0]])
                    e2 = b.op("scalar", lambda s_: s_.activation(
                        out=u1_scr[:, 256:512], in_=u1_scr[:, 0:256], func=AF.Exp),
                        deps=[e1])
                    prev_e2k = e2
                    e4 = b.op("vector", lambda v, src=src, dst_sl=dst_sl, ktm=ktm:
                              v.scalar_tensor_tensor(
                                  out=ktm[:, :, dst_sl:dst_sl + 32],
                                  in0=src.rearrange("p (h w) -> p h w", w=32)[:, 0:8],
                                  scalar=0.0,
                                  in1=u1_scr[:, 256:512].rearrange(
                                      "p (h w) -> p h w", w=32)[:, 0:8],
                                  op0=ALU.max, op1=ALU.add), deps=[e2])
                    ke_all.append(e4)
            k_done = b._merge(ke_all)
            scr_tok[0] = b._merge([k_done, prev_e2k])
            pend[tt] = {"k_done": k_done, "vc": b._merge([vc0, vc1]),
                        "q_elu": q_elu, "vq": b._merge([vq, q_rope])}
        emit_tail(TT - 1)
        proj_done = b._merge([kv_mm_tok[0], tp_last[0], scr_tok[0], v1cp_last[0]])
        b.tap("qf_fm", qf_fm, [128, DT * T], proj_done, dtype=BF16)
        b.tap("kf_last", kf_tt, [128, H * HD], proj_done, dtype=BF16)
        b.tap("v_last", v_tt, [128, H * 65 + 8], proj_done, dtype=BF16)

        kv_ev = []
        for h in range(H):
            base = (h % 2) * 64
            tk = b.op("scalar", lambda s_, h=h, base=base: s_.copy(
                out=kvks[base:base + 64, (h // 2) * 66: (h // 2) * 66 + 65],
                in_=kv_ps(h)), deps=[kv_mm_tok[0]] + kv_ev[-1:])
            kv_ev.append(tk)
        kv_done = b._merge(kv_ev)
        b.tap("kvks", kvks, [128, 8 * 66], kv_done, dtype=BF16)

        out_toks = []
        NP = 3
        ops_slot = [psA[:, 0:512], psA[:, 512:1024], psD[:, 0:512]]
        rb_slot = [psB[:, 0:512], psB[:, 512:1024], psC[:, 0:512]]
        scr_slot = [t_scr, u1_scr, sg_scr]
        ps_free = {p: kv_done for p in range(NP)}
        rb_free = {p: kv_done for p in range(NP)}
        dn_scr = {p: b._merge([kv_done, scr_tok[0]]) for p in range(NP)}
        gidx = 0
        for h in range(H):
            odd = h % 2
            base = odd * 64
            kcol = (h // 2) * 66
            for ch in range(4):
                par = gidx % NP
                gidx += 1
                f32s = scr_slot[par]
                qsl = qf_fm[base:base + 64,
                            (h // 2) * T + ch * 512:(h // 2) * T + (ch + 1) * 512]
                ops_ = ops_slot[par]
                if not odd:
                    def omm(t, kcol=kcol, ops_=ops_, qsl=qsl):
                        return t.matmul(ops_[0:65], kvks[0:64, kcol:kcol + 65], qsl,
                                        start=True, stop=True)
                    ovp = ops_[0:64]
                else:
                    def omm(t, kcol=kcol, ops_=ops_, qsl=qsl):
                        t.matmul(ops_[64:128], kvks[64:128, kcol:kcol + 64], qsl,
                                 start=True, stop=True, tile_position=(64, 64))
                        return t.matmul(ops_[0:1], kvks[64:128, kcol + 64:kcol + 65],
                                        qsl, start=False, stop=True,
                                        tile_position=(64, 0))
                    ovp = ops_[64:128]
                tm_ = b.op("tensor", omm, deps=[kv_done, ps_free[par], tp_last[0]])
                dw = 65 if not odd else 32
                drow = 64 if not odd else 0
                d1 = b.op("vector", lambda v, ops_=ops_, dw=dw, f32s=f32s: v.tensor_scalar(
                    out=f32s[0:dw, 0:512], in0=ops_[0:dw], scalar1=1e-6, scalar2=None,
                    op0=ALU.max), deps=[tm_, dn_scr[par]])
                d1b = b.op("scalar", lambda s_, dw=dw, f32s=f32s: s_.activation(
                    out=f32s[0:dw, 0:512], in_=f32s[0:dw, 0:512], func=AF.Ln),
                    deps=[d1])
                d2 = b.op("scalar", lambda s_, dw=dw, f32s=f32s: s_.activation(
                    out=f32s[0:dw, 0:512], in_=f32s[0:dw, 0:512], func=AF.Exp,
                    scale=-1.0), deps=[d1b])
                rbp = rb_slot[par][64:128] if odd else rb_slot[par][0:64]
                tpz = (64, 0) if not odd else (0, 64)
                rb = b.op("tensor", lambda t, rbp=rbp, f32s=f32s, drow=drow, tpz=tpz:
                          t.matmul(rbp, onesf32[drow:drow + 1, 0:64],
                                   f32s[drow:drow + 1, 0:512],
                                   start=True, stop=True, tile_position=tpz),
                          deps=[d2, t_ones3, rb_free[par]])
                o16 = b.op("scalar", lambda s_, ovp=ovp, base=base, f32s=f32s: s_.copy(
                    out=f32s[base:base + 64, 512:1024], in_=ovp),
                    deps=[tm_, dn_scr[par]])
                ps_free[par] = b._merge([o16, d2])
                dst = out_all[base:base + 64,
                              (h // 2) * T + ch * 512:(h // 2) * T + (ch + 1) * 512]
                osc = b.op("vector", lambda v, dst=dst, rbp=rbp, base=base, f32s=f32s:
                           v.tensor_tensor(
                               out=dst, in0=f32s[base:base + 64, 512:1024], in1=rbp,
                               op=ALU.mult), deps=[rb, o16, proj_done])
                rb_free[par] = osc
                dn_scr[par] = osc
                out_toks.append(osc)
        out_done = b._merge(out_toks)

        x_toks = []
        op_free = {0: out_done, 1: out_done}
        for tt in range(TT):
            par = tt % 2
            pp = pairs[2 + par]
            mm_toks = []
            for dt in range(DT):
                sl = dt % 2
                two = b.dma("sync", at_wo[sl], woutT_blk[dt],
                            deps=[blk_toks[("wout", dt)], slotf[("wo", sl)]],
                            stream=f"wd{sl}")
                def pmm(t, tt=tt, dt=dt, sl=sl, pp=pp):
                    osl = out_all[:, dt * T + tt * 128: dt * T + tt * 128 + 128]
                    st, sp = (dt == 0), (dt == DT - 1)
                    t.matmul(pp[:, 0:512], osl, at_wo[sl][:, 0:512], start=st, stop=sp)
                    return t.matmul(pp[:, 512:1024], osl, at_wo[sl][:, 512:1024],
                                    start=st, stop=sp)
                dep = [two, out_done] + ([mm_toks[-2]] if len(mm_toks) >= 2
                                         else [op_free[par]])
                tok = b.op("tensor", pmm, deps=dep)
                mm_toks.append(tok)
                slotf[("wo", sl)] = tok
            tmm = mm_toks[-1]
            ev = []
            for n in range(2):
                xsl = x_sb[:, tt * D + n * 512: tt * D + n * 512 + 512]
                te = b.op("vector", lambda v, xsl=xsl, pp=pp, n=n: v.tensor_add(
                    out=xsl, in0=xsl, in1=pp[:, n * 512:(n + 1) * 512]), deps=[tmm])
                ev.append(te)
            op_free[par] = b._merge(ev)
            x_toks.extend(ev)
        x_done = b._merge(x_toks)
        gate = b._merge([x_done, proj_done, kv_done, out_done, op_free[0], op_free[1],
                         slotf[("wo", 0)], slotf[("wo", 1)]])
        return x_done, gate

    x_tok = [None]
    st_toks = []
    gate = [prologue_compute]
    arena = [None]
    for seq in range(SEQ):
        if seq == 0:
            t_x = early_x0
            for ee in range(min(4, n_experts)):
                emit_pro_cast(ee)
        else:
            t_x = b.dma("gpsimd", x_sb.rearrange("p (tt d) -> p tt d", d=D),
                        x_in[seq].rearrange("(tt p) d -> p tt d", p=128),
                        deps=[x_tok[0]], stream="xin")
        ln1 = emit_ln([t_x] + gate)
        b.tap(f"h1_s{seq}", h_fm, [128, DT * T], ln1, dtype=BF16)
        rt1 = emit_router([ln1, bias_done])
        if seq == 0:
            emit_rope_tables()
        b.tap(f"comb1_s{seq}", comb_sb, [128, TT * 8], rt1)
        done = rt1
        if seq == 0:
            emit_pro_expert(0)
            emit_pro_expert(1)

        def pf(e, pidx, seq=seq):
            if seq != 0:
                return
            if pidx < 0:
                if e + 4 < n_experts:
                    emit_pro_cast(e + 4)
                if e + 2 < n_experts:
                    queue_pro_expert(e + 2)
                if e == min(2, n_experts - 1) and do_attn:
                    emit_pro_attn_cast()
                if e == min(4, n_experts - 1) and do_attn:
                    emit_pro_attn()
                return
            pro_step(3)

        if do_ffn1:
            ffn1, g1 = emit_ffn([ln1, rt1], arena[0],
                                prefetch_cb=(pf if seq == 0 else None))
            arena[0] = g1
            gate = [g1]
            done = ffn1
            b.tap(f"xf1_s{seq}", x_sb, [128, TT * D], ffn1)
        if do_attn and ("wq", 0) not in blk_toks and not pro_queue:
            emit_pro_attn_cast()
            emit_pro_attn()
        while pro_queue:
            pro_queue.pop(0)()
        if do_attn:
            ln2 = emit_ln([done] + gate)
            at1, g2 = emit_attn([ln2], arena[0])
            arena[0] = g2
            gate = [g2]
            done = at1
            b.tap(f"xa_s{seq}", x_sb, [128, TT * D], at1)
        if do_ffn2:
            ln3 = emit_ln([done] + gate)
            rt2 = emit_router([ln3, bias_done])
            ffn2, g3 = emit_ffn([ln3, rt2], arena[0])
            arena[0] = g3
            gate = [g3]
            done = ffn2
        t_st = b.dma("sync", out[seq].rearrange("(tt p) d -> p tt d", p=128),
                     x_sb.rearrange("p (tt d) -> p tt d", d=D),
                     deps=[done, t_x] + st_toks[-1:], stream="xout")
        st_toks.append(t_st)
        x_tok[0] = t_st

    fin = b._merge(st_toks + [b.tok_all()])
    b.op("sync", lambda s: None, deps=[fin])

    b.finalize()
    return nc, b


_cache = {}


def kernel(**inputs):
    if "main" not in _cache:
        _cache["main"] = build_program()
    nc, b = _cache["main"]
    x = np.ascontiguousarray(inputs["L0_attn_x"])
    in_maps = []
    for c in range(CORES):
        m = {k: np.ascontiguousarray(v) for k, v in inputs.items() if k != "L0_attn_x"}
        m["L0_attn_x"] = x[c * SEQ:(c + 1) * SEQ]
        in_maps.append(m)
    res = run_bass_kernel_spmd(nc, in_maps, core_ids=list(range(CORES)))
    return np.concatenate([np.asarray(r["out"]) for r in res.results], axis=0)


# version 17
# speedup vs baseline: 1.0523x; 1.0523x over previous
import numpy as np
import concourse.bass as bass
import concourse.mybir as mybir
from concourse.bass_utils import run_bass_kernel_spmd

F32 = mybir.dt.float32
BF16 = mybir.dt.bfloat16
I32 = mybir.dt.int32
AF = mybir.ActivationFunctionType
ALU = mybir.AluOpType
AXX = mybir.AxisListType.X

NB, T, D = 16, 2048, 1024
H, HD = 16, 64
E, II = 8, 2048
CORES = 8
SEQ = 2
TT = T // 128
DT = D // 128
KIT = II // 128
COLS = 2 * II // 128
PI = float(np.pi)
TWO_PI = float(2 * np.pi)
LOG10000 = float(np.log(10000.0))


class Builder:
    ENGINES = ("tensor", "vector", "scalar", "gpsimd", "sync")

    def __init__(self, nc, debug_taps=()):
        self.nc = nc
        self.debug_taps = set(debug_taps)
        self.ops = {e: [] for e in self.ENGINES}
        self.sems = {e: nc.alloc_semaphore(f"sem_{e}") for e in self.ENGINES}
        self.counts = {s: 0 for s in self.sems}
        self.dbg_outs = {}

    def op(self, eng, fn, deps=None):
        waits = self._merge(deps)
        self.counts[eng] += 1
        self.ops[eng].append((fn, waits, (self.sems[eng], 1)))
        return {eng: self.counts[eng]}

    def dma(self, eng, out, in_, deps=None, stream="misc", **kw):
        waits = self._merge(deps)
        if stream not in self.sems:
            self.sems[stream] = self.nc.alloc_semaphore(f"s_{stream}")
            self.counts[stream] = 0
        self.counts[stream] += 16
        def fn(e, out=out, in_=in_, kw=kw):
            return e.dma_start(out=out, in_=in_, **kw)
        self.ops[eng].append((fn, waits, (self.sems[stream], 16)))
        return {stream: self.counts[stream]}

    def _merge(self, deps):
        waits = {}
        for d in (deps or []):
            if not d:
                continue
            for k, v in d.items():
                if waits.get(k, -1) < v:
                    waits[k] = v
        return waits

    def tok_all(self, *names):
        names = names or list(self.counts)
        return {n: self.counts[n] for n in names if self.counts[n] > 0}

    def sem_of(self, name):
        return self.sems[name]

    def finalize(self):
        with self.nc.Block() as blk:
            for eng in self.ENGINES:
                entries = self.ops[eng]
                def body(engine, entries=entries):
                    waited = {}
                    for fn, waits, inc in entries:
                        for s, v in waits.items():
                            if waited.get(s, 0) < v:
                                engine.wait_ge(self.sem_of(s), v)
                                waited[s] = v
                        inst = fn(engine)
                        sem, n = inc
                        if inst is not None:
                            inst.then_inc(sem, n)
                        else:
                            engine.sem_inc(sem, n)
                getattr(blk, eng)(body)

    def tap(self, name, src_ap, shape, dep, dtype=F32):
        if name not in self.debug_taps or name in self.dbg_outs:
            return None
        t = self.nc.dram_tensor(f"dbg_{name}", list(shape), dtype, kind="ExternalOutput")
        self.dbg_outs[name] = t
        return self.dma("gpsimd", t.ap(), src_ap, deps=[dep], stream="tap")


def build_program(debug_taps=(), n_experts=E, do_ffn1=True, do_attn=True, do_ffn2=True):
    nc = bass.Bass("TRN2", target_bir_lowering=False)
    b = Builder(nc, debug_taps)

    x_in = nc.dram_tensor("L0_attn_x", [SEQ, T, D], F32, kind="ExternalInput").ap()
    w_qkv = nc.dram_tensor("L0_attn_w_qkv", [3 * D, D], F32, kind="ExternalInput").ap()
    w_out = nc.dram_tensor("L0_attn_w_out", [D, D], F32, kind="ExternalInput").ap()
    w_r = nc.dram_tensor("L0_ffn_router_weight", [E, D], F32, kind="ExternalInput").ap()
    b_r = nc.dram_tensor("L0_ffn_router_bias", [E], F32, kind="ExternalInput").ap()
    w_gu = nc.dram_tensor("L0_ffn_gate_up_proj", [E, 2 * II, D], F32, kind="ExternalInput").ap()
    b_gu = nc.dram_tensor("L0_ffn_gate_up_proj_bias", [E, 2 * II], F32, kind="ExternalInput").ap()
    w_d = nc.dram_tensor("L0_ffn_down_proj", [E, D, II], F32, kind="ExternalInput").ap()
    b_d = nc.dram_tensor("L0_ffn_down_proj_bias", [E, D], F32, kind="ExternalInput").ap()
    out = nc.dram_tensor("out", [SEQ, T, D], F32, kind="ExternalOutput").ap()

    gu_nat = nc.dram_tensor("gu_nat", [E, 2 * II, D], BF16).ap()
    wd_nat = nc.dram_tensor("wd_nat", [E, D, II], BF16).ap()
    qkv_nat = nc.dram_tensor("qkv_nat", [3 * D, D], BF16).ap()
    wout_nat = nc.dram_tensor("wout_nat", [D, D], BF16).ap()
    guT_blk = nc.dram_tensor("guT_blk", [E, COLS, 128, DT * 128], BF16).ap()
    wdT_blk = nc.dram_tensor("wdT_blk", [E, 2, 128, KIT, 512], BF16).ap()
    wqT_blk = nc.dram_tensor("wqT_blk", [DT, 128, D], BF16).ap()
    wkvT_blk = nc.dram_tensor("wkvT_blk", [DT, 128, 2 * D], BF16).ap()
    woutT_blk = nc.dram_tensor("woutT_blk", [DT, 128, D], BF16).ap()

    A = nc.alloc_sbuf_tensor
    x_sb = A("x_sb", [128, TT * D], F32).ap()
    h_fm = A("h_fm", [128, DT * T], BF16).ap()
    act_sb = A("act_sb", [128, 8 * T], BF16).ap()
    wstream = A("wstream", [128, 12288], BF16).ap()
    stage2 = A("stage2", [128, 4096], BF16, is_transpose_dest=True).ap()
    stages = None
    wgu_slot = [wstream[:, i * 1024:(i + 1) * 1024] for i in range(4)]
    wd_slot = [wstream[:, 4096 + i * 4096: 4096 + (i + 1) * 4096] for i in range(2)]
    wq_slot = [wstream[:, i * 1024:(i + 1) * 1024] for i in range(2)]
    wkv_slot = [wstream[:, 2048 + i * 2048: 2048 + (i + 1) * 2048] for i in range(2)]
    wout_slot = [wstream[:, 6144 + i * 1024: 6144 + (i + 1) * 1024] for i in range(2)]
    ident16 = A("ident16", [128, 128], BF16).ap()
    htm_scr = A("htm_scr", [128, D], BF16).ap()
    sg_scr = A("sg_scr", [128, 1024], F32).ap()
    u1_scr = A("u1_scr", [128, 1024], F32).ap()
    t_scr = A("t_scr", [128, 1024], F32).ap()
    sq_scr = t_scr
    cols_sb = A("cols_sb", [128, 64], F32).ap()
    iot_i = A("iot_i", [128, 132], I32).ap()
    iot_f = A("iot_f", [128, 132], F32).ap()
    cos_tm = A("cos_tm", [128, TT * 32], F32).ap()
    sin_tm = A("sin_tm", [128, TT * 32], F32).ap()
    logit_sb = A("logit_sb", [128, TT * 8], F32).ap()
    comb_sb = A("comb_sb", [128, TT * 8], F32).ap()
    comb16 = A("comb16", [128, 8], BF16).ap()
    comb_em = A("comb_em", [8, T], BF16).ap()
    bgcol = A("bgcol", [128, E * 16], F32).ap()
    bucol = A("bucol", [128, E * 16], F32).ap()
    brow_f = A("brow_f", [1, 1024], F32).ap()
    bd16 = A("bd16", [8, D], BF16).ap()
    wr16 = A("wr16", [8, D], BF16).ap()
    wrT = A("wrT", [128, DT * 8], BF16).ap()
    br16 = A("br16", [1, 8], BF16).ap()
    ones16 = A("ones16", [1, 128], BF16).ap()
    onesall16 = A("onesall16", [128, 64], BF16).ap()
    onesf32 = A("onesf32", [128, 64], F32).ap()
    kf_tt = A("kf_tt", [128, H * HD], BF16).ap()
    qf_tm_scr = A("qf_tm", [128, H * HD], BF16).ap()
    v_tt = A("v_tt", [128, H * 65 + 8], BF16).ap()
    kvks = A("kvks", [128, 8 * 66], BF16).ap()
    qsc_bf = A("qsc_bf", [128, 1024], BF16).ap()
    qf_tm2 = A("qf_tm2", [128, H * HD], BF16).ap()
    stages = [stage2[:, 0:2048], stage2[:, 2048:4096], kf_tt, v_tt[:, 0:1024]]
    qf_fm = act_sb
    out_all = h_fm

    P = nc.alloc_psum_tensor
    psA = P("psA", [128, 1024], F32).ap()
    psB = P("psB", [128, 1024], F32).ap()
    psC = P("psC", [128, 1024], F32).ap()
    psD = P("psD", [128, 1024], F32).ap()
    pairs = [psA, psB, psC, psD]

    t_i1 = b.op("gpsimd", lambda g: g.iota(iot_i[:, 0:1], pattern=[[0, 1]], channel_multiplier=1))
    t_i2 = b.op("gpsimd", lambda g: g.iota(iot_i[:, 1:129], pattern=[[1, 128]], channel_multiplier=0))
    t_if = b.op("vector", lambda v: v.tensor_copy(out=iot_f[:, 0:129], in_=iot_i[:, 0:129]),
                deps=[t_i1, t_i2])
    t_one_f = b.op("vector", lambda v: v.memset(iot_f[:, 130:131], 1.0))
    t_id = b.op("vector", lambda v: v.tensor_scalar(
        out=ident16, in0=iot_f[:, 1:129], scalar1=iot_f[:, 0:1], scalar2=None,
        op0=ALU.is_equal), deps=[t_if])
    t_ones = b.op("vector", lambda v: v.memset(ones16, 1.0))
    t_ones2 = b.op("vector", lambda v: v.memset(onesall16, 1.0))
    t_ones3 = b.op("vector", lambda v: v.memset(onesf32, 1.0))
    t_eps = b.op("vector", lambda v: v.memset(cols_sb[:, 46:47], 1e-5))
    t_b11 = b.op("vector", lambda v: v.memset(cols_sb[:, 47:48], float(1.702 * 7.0)))
    def emit_rope_tables():
        t_efr = b.op("scalar", lambda s: s.activation(
            out=cols_sb[:, 0:32], in_=iot_f[:, 1:33], func=AF.Exp, scale=-LOG10000 / 32.0),
            deps=[t_if])

        rope_last = [t_efr, t_if]
        for tt in range(TT):
            c0 = tt * 32
            tc = b.op("vector", lambda v, tt=tt: v.tensor_scalar(
                out=cols_sb[:, 33:34], in0=iot_f[:, 0:1], scalar1=float(128 * tt),
                scalar2=None, op0=ALU.add), deps=rope_last)
            ta = b.op("vector", lambda v, c0=c0: v.tensor_scalar(
                out=sin_tm[:, c0:c0 + 32], in0=cols_sb[:, 0:32], scalar1=cols_sb[:, 33:34],
                scalar2=None, op0=ALU.mult), deps=[tc])
            t1 = b.op("vector", lambda v, c0=c0: v.tensor_scalar(
                out=cos_tm[:, c0:c0 + 32], in0=sin_tm[:, c0:c0 + 32], scalar1=1.0 / TWO_PI,
                scalar2=None, op0=ALU.mult), deps=[ta])
            t2 = b.op("vector", lambda v, c0=c0: v.tensor_copy(
                out=iot_i[:, 100:132], in_=cos_tm[:, c0:c0 + 32]), deps=[t1])
            t3 = b.op("vector", lambda v, c0=c0: v.tensor_copy(
                out=cos_tm[:, c0:c0 + 32], in_=iot_i[:, 100:132]), deps=[t2])
            t4 = b.op("vector", lambda v, c0=c0: v.scalar_tensor_tensor(
                out=sin_tm[:, c0:c0 + 32], in0=cos_tm[:, c0:c0 + 32], scalar=-TWO_PI,
                in1=sin_tm[:, c0:c0 + 32], op0=ALU.mult, op1=ALU.add), deps=[t3])
            t4b = b.op("vector", lambda v, c0=c0: v.tensor_scalar(
                out=iot_f[:, 96:128], in0=sin_tm[:, c0:c0 + 32], scalar1=PI,
                scalar2=TWO_PI, op0=ALU.is_gt, op1=ALU.mult), deps=[t4])
            t4 = b.op("vector", lambda v, c0=c0: v.tensor_tensor(
                out=sin_tm[:, c0:c0 + 32], in0=sin_tm[:, c0:c0 + 32],
                in1=iot_f[:, 96:128], op=ALU.subtract), deps=[t4b])
            t5 = b.op("vector", lambda v, c0=c0: v.tensor_scalar(
                out=cos_tm[:, c0:c0 + 32], in0=sin_tm[:, c0:c0 + 32], scalar1=PI / 2,
                scalar2=None, op0=ALU.add), deps=[t4])
            t6 = b.op("vector", lambda v, c0=c0: v.tensor_scalar(
                out=iot_f[:, 96:128], in0=cos_tm[:, c0:c0 + 32], scalar1=PI,
                scalar2=TWO_PI, op0=ALU.is_gt, op1=ALU.mult), deps=[t5])
            t7 = b.op("vector", lambda v, c0=c0: v.tensor_tensor(
                out=cos_tm[:, c0:c0 + 32], in0=cos_tm[:, c0:c0 + 32],
                in1=iot_f[:, 96:128], op=ALU.subtract), deps=[t6])
            t8 = b.op("scalar", lambda s, c0=c0: s.activation(
                out=sin_tm[:, c0:c0 + 32], in_=sin_tm[:, c0:c0 + 32], func=AF.Sin), deps=[t4, t5])
            t9 = b.op("scalar", lambda s, c0=c0: s.activation(
                out=cos_tm[:, c0:c0 + 32], in_=cos_tm[:, c0:c0 + 32], func=AF.Sin), deps=[t7])
            rope_last = [t8, t9]
        rope_done = b._merge(rope_last)
        return rope_done


    t_wr = b.dma("gpsimd", wr16, w_r, stream="wr")
    t_brr = b.dma("gpsimd", br16, b_r.unsqueeze(0), stream="br")
    t_bd = b.dma("gpsimd", bd16, b_d, stream="bd")

    wrT_done = []
    for dt in range(DT):
        tp = b.op("tensor", lambda t, dt=dt: t.transpose(
            psA.bitcast(BF16)[:, 0:8], wr16[:, dt * 128:(dt + 1) * 128], ident16[0:8, 0:8]),
            deps=[t_wr, t_id] + wrT_done[-1:])
        te = b.op("vector", lambda v, dt=dt: v.tensor_copy(
            out=wrT[:, dt * 8:(dt + 1) * 8], in_=psA.bitcast(BF16)[:, 0:8]), deps=[tp])
        wrT_done.append(te)
    wrT_done = b._merge(wrT_done)

    b_iter = [(e, half, p8) for e in range(n_experts) for half in range(2)
              for p8 in range(2)]
    tl_toks = {}
    mm_toks_b = {}
    ev_toks_b = {}
    for i, (e, half, p8) in enumerate(b_iter):
        par = i % 2
        off = half * 2048 + p8 * 1024
        tla = b.dma("sync", brow_f[0:1, 0:512],
                    b_gu[e].unsqueeze(0)[0:1, off:off + 512],
                    deps=[mm_toks_b.get(i - 1), t_one_f, t_if], stream="brow0")
        tlb = b.dma("sync", brow_f[0:1, 512:1024],
                    b_gu[e].unsqueeze(0)[0:1, off + 512:off + 1024],
                    deps=[mm_toks_b.get(i - 1)], stream="brow0")
        tl_toks[i] = b._merge([tla, tlb])
        def mmf(t, par=par):
            last = None
            for k in range(8):
                src = brow_f[0:1, k * 128:(k + 1) * 128]
                last = t.matmul(psB[:, par * 512 + k: par * 512 + k + 1], src,
                                iot_f[0:1, 130:131], start=(k == 0), stop=(k == 7))
            return last
        mm_toks_b[i] = b.op("tensor", mmf, deps=[tl_toks[i], ev_toks_b.get(i - 2)])
        dst = bgcol if half == 0 else bucol
        def evf(v, e=e, half=half, p8=p8, dst=dst, par=par):
            c0 = e * 16 + p8 * 8
            if half == 0:
                return v.tensor_scalar(out=dst[:, c0:c0 + 8],
                                       in0=psB[:, par * 512: par * 512 + 8],
                                       scalar1=-1.0, scalar2=7.0,
                                       op0=ALU.mult, op1=ALU.add)
            return v.tensor_scalar(out=dst[:, c0:c0 + 8],
                                   in0=psB[:, par * 512: par * 512 + 8],
                                   scalar1=7.0, scalar2=None, op0=ALU.add)
        ev_toks_b[i] = b.op("vector", evf, deps=[mm_toks_b[i]])
    bias_done = b._merge(list(ev_toks_b.values()))
    b.tap("bgcol", bgcol, [128, E * 16], bias_done)
    b.tap("bucol", bucol, [128, E * 16], bias_done)

    early_x0 = b.dma("gpsimd", x_sb.rearrange("p (tt d) -> p tt d", d=D),
                     x_in[0].rearrange("(tt p) d -> p tt d", p=128), stream="xin")

    cast_toks = {}
    stage_free = [None, None, None, None]
    stage_ctr = [0, 0]
    blk_toks = {}

    def transp_unit(ck, src, width, dst, dst_src_re=None, chain=0):
        par = chain * 2 + stage_ctr[chain] % 2
        stage_ctr[chain] += 1
        stg = stages[par]
        tt1 = b.dma("scalar", stg[:, 0:width], src,
                    deps=[cast_toks[ck], stage_free[par]], transpose=True,
                    stream=f"staget{par}")
        if dst_src_re is None:
            src2, dst2 = stg[:, 0:width], dst
            toks = [b.dma("gpsimd", dst2, src2, deps=[tt1], stream=f"stages{par}")]
        else:
            sre, dre, kw = dst_src_re
            src2 = stg[:, 0:width].rearrange(sre, **kw)
            dst2 = dst.rearrange(dre)
            nb = src2.shape[1]
            step = max(1, 512 // 128)
            toks = []
            for i0 in range(0, nb, step):
                i1 = min(nb, i0 + step)
                toks.append(b.dma("gpsimd", dst2[:, i0:i1], src2[:, i0:i1],
                                  deps=[tt1], stream=f"stages{par}"))
        tok = b._merge(toks)
        stage_free[par] = tok
        return tok

    def emit_pro_cast(e):
        cast_toks[("gu", e)] = b.dma("gpsimd", gu_nat[e], w_gu[e],
                                     deps=[cast_toks.get(("gu", e - 2))],
                                     stream=f"castgu{e % 2}")
        cast_toks[("wd", e)] = b.dma("gpsimd", wd_nat[e], w_d[e],
                                     deps=[cast_toks.get(("wd", e - 2))],
                                     stream=f"castwd{e % 2}")

    pro_queue = []

    def _gu_unit(e, dt, jh):
        tok = transp_unit(
            ("gu", e),
            gu_nat[e][jh * 2048:(jh + 1) * 2048, dt * 128:(dt + 1) * 128], 2048,
            guT_blk[e, jh * 16:(jh + 1) * 16, :, dt * 128:(dt + 1) * 128],
            ("p (col jj) -> p col jj", "col p jj -> p col jj", {"jj": 128}))
        key = ("gu", e, dt)
        blk_toks[key] = b._merge([blk_toks.get(key), tok])

    def _wd_unit(e, it):
        blk_toks[("wd", e, it)] = transp_unit(
            ("wd", e),
            wd_nat[e][:, it * 128:(it + 1) * 128], 1024,
            wdT_blk[e, :, :, it, :],
            ("p (dc d) -> p dc d", "dc p d -> p dc d", {"d": 512}), chain=1)

    def queue_pro_expert(e):
        for dt in range(DT):
            for jh in range(2):
                pro_queue.append(lambda e=e, dt=dt, jh=jh: _gu_unit(e, dt, jh))
        for it in range(KIT):
            pro_queue.append(lambda e=e, it=it: _wd_unit(e, it))

    def emit_pro_expert(e):
        queue_pro_expert(e)
        while pro_queue:
            pro_queue.pop(0)()

    def pro_step(n=3):
        for _ in range(n):
            if pro_queue:
                pro_queue.pop(0)()

    def emit_pro_attn_cast():
        cast_toks["qkv"] = b.dma("gpsimd", qkv_nat, w_qkv, stream="castq")
        cast_toks["wout"] = b.dma("gpsimd", wout_nat, w_out, stream="castw")

    def emit_pro_attn():
        def u1(dt):
            blk_toks[("wq", dt)] = transp_unit(
                "qkv", qkv_nat[0:D, dt * 128:(dt + 1) * 128], 1024, wqT_blk[dt], chain=1)
        def u2(dt):
            blk_toks[("wkv", dt)] = transp_unit(
                "qkv", qkv_nat[D:3 * D, dt * 128:(dt + 1) * 128], 2048, wkvT_blk[dt])
        def u3(dt):
            blk_toks[("wout", dt)] = transp_unit(
                "wout", wout_nat[:, dt * 128:(dt + 1) * 128], 1024, woutT_blk[dt], chain=1)
        for dt in range(DT):
            pro_queue.append(lambda dt=dt: u1(dt))
            pro_queue.append(lambda dt=dt: u2(dt))
            pro_queue.append(lambda dt=dt: u3(dt))

    prologue_compute = b.tok_all("tensor", "vector", "scalar", "gpsimd")

    def emit_ln(deps):
        toks = []
        serial = []
        prev_tp = []
        for tt in range(TT):
            xt = x_sb[:, tt * D:(tt + 1) * D]
            t_s = b.op("vector", lambda v, xt=xt: v.reduce_sum(
                out=cols_sb[:, 40:41], in_=xt, axis=AXX), deps=deps + serial[-1:])
            t_q = b.op("scalar", lambda s, xt=xt: s.activation(
                out=sq_scr, in_=xt, func=AF.Square, accum_out=cols_sb[:, 41:42]),
                deps=deps + serial[-1:])
            t_m = b.op("vector", lambda v: v.tensor_scalar(
                out=cols_sb[:, 42:43], in0=cols_sb[:, 40:41], scalar1=1.0 / D,
                scalar2=None, op0=ALU.mult), deps=[t_s])
            t_v = b.op("vector", lambda v: v.scalar_tensor_tensor(
                out=cols_sb[:, 43:44], in0=cols_sb[:, 42:43], scalar=-1.0,
                in1=cols_sb[:, 42:43], op0=ALU.mult, op1=ALU.mult), deps=[t_m])
            t_v2 = b.op("vector", lambda v: v.scalar_tensor_tensor(
                out=cols_sb[:, 43:44], in0=cols_sb[:, 41:42], scalar=1.0 / D,
                in1=cols_sb[:, 43:44], op0=ALU.mult, op1=ALU.add), deps=[t_v, t_q])
            t_sd = b.op("scalar", lambda s: s.activation(
                out=cols_sb[:, 44:45], in_=cols_sb[:, 43:44], func=AF.Sqrt,
                bias=cols_sb[:, 46:47]), deps=[t_v2, t_eps])
            t_r = b.op("vector", lambda v: v.reciprocal(
                out=cols_sb[:, 45:46], in_=cols_sb[:, 44:45]), deps=[t_sd])
            t_h = b.op("vector", lambda v, xt=xt: v.tensor_scalar(
                out=htm_scr, in0=xt, scalar1=cols_sb[:, 42:43], scalar2=cols_sb[:, 45:46],
                op0=ALU.subtract, op1=ALU.mult), deps=[t_r])
            serial.append(t_h)
            for dt in range(DT):
                ps = pairs[dt % 2].bitcast(BF16)[:, 0:128]
                t_tp = b.op("tensor", lambda t, dt=dt, ps=ps: t.transpose(
                    ps, htm_scr[:, dt * 128:(dt + 1) * 128], ident16),
                    deps=[t_h, t_id] + prev_tp[-2:])
                t_ev = b.op("vector", lambda v, dt=dt, tt=tt, ps=ps: v.tensor_copy(
                    out=h_fm[:, dt * T + tt * 128: dt * T + tt * 128 + 128], in_=ps),
                    deps=[t_tp])
                prev_tp.append(t_ev)
                toks.append(t_ev)
        return b._merge(toks)

    def emit_router(deps):
        toks = []
        prev = []
        for tt in range(TT):
            def mmf(t, tt=tt):
                for dt in range(DT):
                    t.matmul(psC[:, 0:8],
                             h_fm[:, dt * T + tt * 128: dt * T + tt * 128 + 128],
                             wrT[:, dt * 8:(dt + 1) * 8],
                             start=(dt == 0), stop=False)
                return t.matmul(psC[:, 0:8], ones16, br16, start=False, stop=True)
            t_mm = b.op("tensor", mmf, deps=deps + [wrT_done, t_brr, t_ones] + prev[-1:])
            t_cp = b.op("vector", lambda v, tt=tt: v.tensor_copy(
                out=logit_sb[:, tt * 8:(tt + 1) * 8], in_=psC[:, 0:8]), deps=[t_mm])
            prev.append(t_cp)
        ltok = b._merge(prev)
        prev = []
        for tt in range(TT):
            lg = logit_sb[:, tt * 8:(tt + 1) * 8]
            cb = comb_sb[:, tt * 8:(tt + 1) * 8]
            t_m8 = b.op("vector", lambda v, lg=lg: v.max(out=cols_sb[:, 48:56], in_=lg),
                        deps=[ltok] + prev[-1:])
            t_nm = b.op("vector", lambda v: v.tensor_scalar(
                out=cols_sb[:, 56:57], in0=cols_sb[:, 48:49], scalar1=-1.0, scalar2=None,
                op0=ALU.mult), deps=[t_m8])
            t_e = b.op("scalar", lambda s, lg=lg, cb=cb: s.activation(
                out=cb, in_=lg, func=AF.Exp, bias=cols_sb[:, 56:57]), deps=[t_nm])
            t_msk = b.op("vector", lambda v, lg=lg: v.tensor_scalar(
                out=t_scr[:, 32:40], in0=lg, scalar1=cols_sb[:, 51:52], scalar2=None,
                op0=ALU.is_ge), deps=[t_m8])
            t_em = b.op("vector", lambda v, cb=cb: v.tensor_tensor(
                out=cb, in0=cb, in1=t_scr[:, 32:40], op=ALU.mult), deps=[t_msk, t_e])
            t_sum = b.op("vector", lambda v, cb=cb: v.reduce_sum(
                out=cols_sb[:, 57:58], in_=cb, axis=AXX), deps=[t_em])
            t_rc = b.op("vector", lambda v: v.reciprocal(
                out=cols_sb[:, 58:59], in_=cols_sb[:, 57:58]), deps=[t_sum])
            t_cb = b.op("vector", lambda v, cb=cb: v.tensor_scalar(
                out=cb, in0=cb, scalar1=cols_sb[:, 58:59], scalar2=0.5,
                op0=ALU.mult, op1=ALU.mult), deps=[t_rc])
            t_c16 = b.op("vector", lambda v, cb=cb: v.tensor_copy(out=comb16, in_=cb),
                         deps=[t_cb])
            t_tp = b.op("tensor", lambda t: t.transpose(
                psD.bitcast(BF16)[0:8, 0:128], comb16, ident16),
                deps=[t_c16, t_id] + prev[-1:])
            t_em2 = b.op("vector", lambda v, tt=tt: v.tensor_copy(
                out=comb_em[:, tt * 128:(tt + 1) * 128],
                in_=psD.bitcast(BF16)[0:8, 0:128]), deps=[t_tp])
            prev.append(b._merge([t_cb, t_em2]))
            toks.append(prev[-1])
        return b._merge(toks)

    def gu_blk_done(e):
        return b._merge([blk_toks[("gu", e, dt)] for dt in range(DT)])

    def wd_blk_done(e):
        return b._merge([blk_toks[("wd", e, it)] for it in range(KIT)])

    def emit_ffn(deps, arena_free, prefetch_cb=None):
        x_toks = []
        slot_free = {("gu", 0): arena_free, ("gu", 1): arena_free,
                     ("gu", 2): arena_free, ("gu", 3): arena_free,
                     ("wd", 0): arena_free, ("wd", 1): arena_free}
        gu_pair_ctr = [0]
        pg_free = {0: None, 1: None}
        dn_free = {0: None, 1: None}
        scr_free = [None]
        u_free = [None]
        act_ready = {}
        act_pe_done = [None]

        for e in range(n_experts):
            if prefetch_cb:
                prefetch_cb(e, -1)
            for half in range(2):
                ap_done = act_pe_done[0]
                wd_toks = {}
                for dchunk in range(2):
                    sl = dchunk
                    wd_toks[dchunk] = b.dma(
                        "sync", wd_slot[sl].rearrange("p (k d) -> p k d", d=512),
                        wdT_blk[e, dchunk, :, half * 8:(half + 1) * 8, :],
                        deps=[wd_blk_done(e), slot_free[("wd", sl)]],
                        stream=f"wd{sl}")
                for p_idx in range(8):
                    if prefetch_cb:
                        prefetch_cb(e, half * 8 + p_idx)
                    gcol = half * 8 + p_idx
                    ucol = 16 + half * 8 + p_idx
                    par = gu_pair_ctr[0] % 2
                    gsl, usl = par, 2 + par
                    tg = b.dma("sync", wgu_slot[gsl], guT_blk[e, gcol],
                               deps=deps + [gu_blk_done(e), slot_free[("gu", gsl)]],
                               stream=f"wgu{gsl}")
                    tu = b.dma("sync", wgu_slot[usl], guT_blk[e, ucol],
                               deps=deps + [gu_blk_done(e), slot_free[("gu", usl)]],
                               stream=f"wgu{usl}")
                    gu_pair_ctr[0] += 1
                    pe_toks = {}
                    for c in range(2):
                        pg, pu = pairs[2 * c], pairs[2 * c + 1]
                        def mmf(t, gsl=gsl, usl=usl, c=c, pg=pg, pu=pu):
                            last = None
                            for which, sl, ps in (("g", gsl, pg), ("u", usl, pu)):
                                for kt in range(DT):
                                    for n in range(2):
                                        rhs = h_fm[:, kt * T + c * 1024 + n * 512:
                                                   kt * T + c * 1024 + (n + 1) * 512]
                                        last = t.matmul(ps[:, n * 512:(n + 1) * 512],
                                                        wgu_slot[sl][:, kt * 128:(kt + 1) * 128],
                                                        rhs, start=(kt == 0), stop=(kt == DT - 1))
                            return last
                        pe_toks[c] = b.op("tensor", mmf,
                                          deps=deps + [tg, tu, pg_free[c], dn_free[c],
                                                       act_pe_done[0]])
                    dbg_tok = None
                    if e == 0 and half == 0 and p_idx == 0 and "gate_ps" in b.debug_taps:
                        dbg_tok = b.op("vector", lambda v: v.tensor_copy(
                            out=htm_scr, in_=pairs[0]), deps=[pe_toks[0]])
                        b.tap("gate_ps", htm_scr, [128, 1024], dbg_tok, dtype=BF16)
                        dbg_tok = b.tok_all("vector", "tap")
                        pe_toks = {0: b._merge([pe_toks[0], dbg_tok]), 1: pe_toks[1]}
                    for c in range(2):
                        pg, pu = pairs[2 * c], pairs[2 * c + 1]
                        bgc = bgcol[:, e * 16 + gcol: e * 16 + gcol + 1]
                        buc = bucol[:, e * 16 + gcol: e * 16 + gcol + 1]
                        a1 = b.op("scalar", lambda s, pg=pg, bgc=bgc: s.activation(
                            out=t_scr, in_=pg, func=AF.Relu, bias=bgc, scale=-1.0),
                            deps=[pe_toks[c], bias_done, scr_free[0]])
                        a2 = b.op("scalar", lambda s: s.activation(
                            out=sg_scr, in_=t_scr, func=AF.Sigmoid, bias=cols_sb[:, 47:48],
                            scale=-1.702), deps=[a1, t_b11])
                        a3 = b.op("scalar", lambda s, pu=pu, buc=buc: s.activation(
                            out=u1_scr, in_=pu, func=AF.Relu, bias=buc), deps=[a2, u_free[0]])
                        pg_free[c] = b._merge([a3, dbg_tok]) if dbg_tok else a3
                        d1 = b.op("vector", lambda v: v.tensor_scalar(
                            out=t_scr, in0=t_scr, scalar1=-1.0, scalar2=7.0,
                            op0=ALU.mult, op1=ALU.add), deps=[a1, a2])
                        d2 = b.op("vector", lambda v: v.tensor_tensor(
                            out=t_scr, in0=t_scr, in1=sg_scr, op=ALU.mult), deps=[d1, a2])
                        d3 = b.op("vector", lambda v: v.tensor_scalar(
                            out=u1_scr, in0=u1_scr, scalar1=14.0, scalar2=-6.0,
                            op0=ALU.min, op1=ALU.add), deps=[a3])
                        dst = act_sb[:, p_idx * T + c * 1024: p_idx * T + (c + 1) * 1024]
                        d4 = b.op("vector", lambda v, dst=dst: v.tensor_tensor(
                            out=dst, in0=t_scr, in1=u1_scr, op=ALU.mult),
                            deps=[d2, d3, ap_done])
                        u_free[0] = d4
                        scr_free[0] = d4
                        act_ready[(p_idx, c)] = d4
                    slot_free[("gu", gsl)] = pe_toks[1]
                    slot_free[("gu", usl)] = pe_toks[1]
                act_all = b._merge([act_ready[k] for k in act_ready])
                if e == 0 and half == 0:
                    b.tap("act_e0h0", act_sb, [128, 8 * T], act_all, dtype=BF16)
                dn_pe = []
                for dchunk in range(2):
                    for ttb in range(4):
                        bp = ttb % 2
                        def mmf(t, e=e, half=half, dchunk=dchunk, ttb=ttb, bp=bp):
                            last = None
                            for kit8 in range(8):
                                for tti in range(4):
                                    tt = ttb * 4 + tti
                                    lhs = act_sb[:, kit8 * T + tt * 128:
                                                 kit8 * T + tt * 128 + 128]
                                    ps = pairs[2 * bp + tti // 2][:, (tti % 2) * 512:
                                                                  (tti % 2 + 1) * 512]
                                    first = (kit8 == 0)
                                    if first and e == 0 and half == 0:
                                        t.matmul(ps, comb_em[:, tt * 128:(tt + 1) * 128],
                                                 bd16[0:8, dchunk * 512:(dchunk + 1) * 512],
                                                 start=True, stop=False)
                                        first = False
                                    last = t.matmul(
                                        ps, lhs,
                                        wd_slot[dchunk][:, kit8 * 512:(kit8 + 1) * 512],
                                        start=first, stop=(kit8 == 7))
                            return last
                        tmm = b.op("tensor", mmf,
                                   deps=[act_all, wd_toks[dchunk], dn_free[bp], pg_free[0],
                                         pg_free[1]])
                        stt_toks = []
                        for tti in range(4):
                            tt = ttb * 4 + tti
                            ps = pairs[2 * bp + tti // 2][:, (tti % 2) * 512:(tti % 2 + 1) * 512]
                            xsl = x_sb[:, tt * D + dchunk * 512: tt * D + dchunk * 512 + 512]
                            ccol = comb_sb[:, tt * 8 + e: tt * 8 + e + 1]
                            tst = b.op("vector", lambda v, ps=ps, xsl=xsl, ccol=ccol:
                                       v.scalar_tensor_tensor(
                                           out=xsl, in0=ps, scalar=ccol, in1=xsl,
                                           op0=ALU.mult, op1=ALU.add), deps=[tmm])
                            stt_toks.append(tst)
                        dn_free[bp] = b._merge(stt_toks)
                        dn_pe.append(tmm)
                        x_toks.extend(stt_toks)
                act_pe_done[0] = b._merge(dn_pe)
                slot_free[("wd", 0)] = dn_pe[3]
                slot_free[("wd", 1)] = dn_pe[7]
        gate = b._merge([act_pe_done[0], pg_free[0], pg_free[1], dn_free[0], dn_free[1],
                         u_free[0], scr_free[0],
                         slot_free[("gu", 0)], slot_free[("gu", 1)],
                         slot_free[("gu", 2)], slot_free[("gu", 3)]])
        return b._merge(x_toks), gate

    at_wq = [wstream[:, i * 1024:(i + 1) * 1024] for i in range(2)]
    at_wkv = [wstream[:, 2048 + i * 2048: 2048 + (i + 1) * 2048] for i in range(2)]
    at_wo = [wstream[:, 6144 + i * 1024: 6144 + (i + 1) * 1024] for i in range(2)]

    def emit_attn(deps, arena_free):
        deps = deps + [stage_free[2], stage_free[3]]
        t_vones = b.op("vector", lambda v: v.memset(
            v_tt[:, 0:H * 65].rearrange("p (h w) -> p h w", w=65)[:, :, 64:65], 1.0),
            deps=deps)
        cos, sin = cos_tm, sin_tm
        slotf = {}
        for nm in ("wq", "wkv", "wo"):
            slotf[(nm, 0)] = arena_free
            slotf[(nm, 1)] = arena_free
        kv_mm_tok = [None]
        tp_last = [None]
        v1cp_last = [None]
        scr_tok = [b._merge(deps)]
        krope_ps = {0: None, 1: None}
        vcp0_last = [None]

        def kv_ps(h):
            pr = h // 2
            base = (h % 2) * 64
            bank = pr // 4
            off = (pr % 4) * 65
            return psC[base:base + 64, bank * 512 + off: bank * 512 + off + 65]

        pend = {}
        qf_tms = [qf_tm_scr, qf_tm2]

        def emit_tail(tt, cur_vq=None):
            st = pend.pop(tt)
            def kvmm(t, tt=tt):
                last = None
                for h in range(H):
                    tp = (0, (h % 2) * 64)
                    first = (tt == 0) and (h in (0, 1, 8, 9))
                    last = t.matmul(kv_ps(h), kf_tt[:, h * 64:(h + 1) * 64],
                                    v_tt[:, h * 65: h * 65 + 65],
                                    start=first, stop=(tt == TT - 1),
                                    tile_position=tp)
                return last
            kv_mm_tok[0] = b.op("tensor", kvmm, deps=[st["k_done"], st["vc"]])
            qtm = qf_tms[tt % 2]
            tps = []
            for dt in range(DT):
                pst = psA.bitcast(BF16)[:, 0:128] if dt % 2 == 0 else \
                    psA.bitcast(BF16)[:, 128:256]
                ttp = b.op("tensor", lambda t, pst=pst, dt=dt, qtm=qtm: t.transpose(
                    pst, qtm[:, dt * 128:(dt + 1) * 128], ident16),
                    deps=[st["q_elu"], t_id, cur_vq or st["vq"]] + tps[-2:])
                tev = b.op("vector", lambda v, pst=pst, dt=dt, tt=tt: v.tensor_copy(
                    out=qf_fm[:, dt * T + tt * 128: dt * T + tt * 128 + 128], in_=pst),
                    deps=[ttp])
                tps.append(tev)
            tp_last[0] = b._merge(tps)

        for tt in range(TT):
            p1_toks = []
            for dt in range(DT):
                sl = dt % 2
                twq = b.dma("sync", at_wq[sl], wqT_blk[dt],
                            deps=[blk_toks[("wq", dt)], slotf[("wq", sl)]],
                            stream=f"wgu{sl}")
                twk = b.dma("sync", at_wkv[sl], wkvT_blk[dt],
                            deps=[blk_toks[("wkv", dt)], slotf[("wkv", sl)]],
                            stream=f"wgu{2 + sl}")
                def p1(t, tt=tt, dt=dt, sl=sl):
                    hsl = h_fm[:, dt * T + tt * 128: dt * T + tt * 128 + 128]
                    st_, sp = (dt == 0), (dt == DT - 1)
                    t.matmul(psA[:, 0:512], hsl, at_wq[sl][:, 0:512], start=st_, stop=sp)
                    t.matmul(psA[:, 512:1024], hsl, at_wq[sl][:, 512:1024], start=st_, stop=sp)
                    t.matmul(psB[:, 0:512], hsl, at_wkv[sl][:, 0:512], start=st_, stop=sp)
                    t.matmul(psB[:, 512:1024], hsl, at_wkv[sl][:, 512:1024], start=st_, stop=sp)
                    t.matmul(psD[:, 0:512], hsl, at_wkv[sl][:, 1024:1536], start=st_, stop=sp)
                    return t.matmul(psD[:, 512:1024], hsl, at_wkv[sl][:, 1536:2048],
                                    start=st_, stop=sp)
                dep1 = [twq, twk] + ([p1_toks[-2]] if len(p1_toks) >= 2 else
                                     [scr_tok[0], tp_last[0], v1cp_last[0],
                                      krope_ps[0], vcp0_last[0]])
                tok = b.op("tensor", p1, deps=dep1)
                p1_toks.append(tok)
                slotf[("wq", sl)] = tok
                slotf[("wkv", sl)] = tok
            p1_done = p1_toks[-1]
            vq = b.op("scalar", lambda s_: s_.copy(out=qsc_bf, in_=psA),
                      deps=[p1_done, scr_tok[0]])
            if tt > 0:
                emit_tail(tt - 1, cur_vq=vq)
            vc0 = b.op("scalar", lambda s_: s_.copy(
                out=v_tt[:, 0:H * 65].rearrange("p (h w) -> p h w", w=65)[:, 0:8, 0:64],
                in_=psD[:, 0:512].rearrange("p (h w) -> p h w", w=64)),
                deps=[p1_done, kv_mm_tok[0]])
            vcp0_last[0] = vc0
            vc1 = b.op("scalar", lambda s_: s_.copy(
                out=v_tt[:, 0:H * 65].rearrange("p (h w) -> p h w", w=65)[:, 8:16, 0:64],
                in_=psD[:, 512:1024].rearrange("p (h w) -> p h w", w=64)),
                deps=[p1_done, kv_mm_tok[0]])
            v1cp_last[0] = vc1
            pq = qsc_bf.rearrange("p (h w) -> p h w", w=64)
            cce = cos[:, tt * 32:(tt + 1) * 32].unsqueeze(1).to_broadcast([128, H, 32])
            sse = sin[:, tt * 32:(tt + 1) * 32].unsqueeze(1).to_broadcast([128, H, 32])
            tv = t_scr.rearrange("p (h w) -> p h w", w=32)
            sv = sg_scr.rearrange("p (h w) -> p h w", w=32)
            r1 = b.op("vector", lambda v, pq=pq, cce=cce: v.scalar_tensor_tensor(
                out=tv[:, 0:16], in0=pq[:, :, 0:32], scalar=0.125, in1=cce,
                op0=ALU.mult, op1=ALU.mult), deps=[vq])
            r2 = b.op("vector", lambda v, pq=pq, sse=sse: v.scalar_tensor_tensor(
                out=tv[:, 16:32], in0=pq[:, :, 32:64], scalar=0.125, in1=sse,
                op0=ALU.mult, op1=ALU.mult), deps=[vq])
            r3 = b.op("vector", lambda v, pq=pq, sse=sse: v.scalar_tensor_tensor(
                out=sv[:, 0:16], in0=pq[:, :, 0:32], scalar=0.125, in1=sse,
                op0=ALU.mult, op1=ALU.mult), deps=[vq])
            r4 = b.op("vector", lambda v, pq=pq, cce=cce: v.scalar_tensor_tensor(
                out=sv[:, 16:32], in0=pq[:, :, 32:64], scalar=0.125, in1=cce,
                op0=ALU.mult, op1=ALU.mult), deps=[vq])
            q_rope = b._merge([r1, r2, r3, r4])
            r5 = b.op("vector", lambda v: v.tensor_sub(
                out=t_scr[:, 0:512], in0=t_scr[:, 0:512], in1=t_scr[:, 512:1024]),
                deps=[r1, r2])
            r6 = b.op("vector", lambda v: v.tensor_add(
                out=sg_scr[:, 0:512], in0=sg_scr[:, 0:512], in1=sg_scr[:, 512:1024]),
                deps=[r3, r4])
            qtm = qf_tms[tt % 2].rearrange("p (h w) -> p h w", w=64)
            el_toks = []
            prev_e2 = None
            for src, dst_sl in ((t_scr, 0), (sg_scr, 32)):
                e1 = b.op("vector", lambda v, src=src: v.tensor_scalar(
                    out=u1_scr[:, 0:512], in0=src[:, 0:512], scalar1=0.0, scalar2=None,
                    op0=ALU.min), deps=[r5, r6, prev_e2])
                e2 = b.op("scalar", lambda s_: s_.activation(
                    out=u1_scr[:, 512:1024], in_=u1_scr[:, 0:512], func=AF.Exp), deps=[e1])
                prev_e2 = e2
                e4 = b.op("vector", lambda v, src=src, dst_sl=dst_sl, qtm=qtm:
                          v.scalar_tensor_tensor(
                              out=qtm[:, :, dst_sl:dst_sl + 32],
                              in0=src.rearrange("p (h w) -> p h w", w=32)[:, 0:16],
                              scalar=0.0,
                              in1=u1_scr[:, 512:1024].rearrange("p (h w) -> p h w", w=32)[:, 0:16],
                              op0=ALU.max, op1=ALU.add), deps=[e2])
                el_toks.append(e4)
            q_elu = b._merge(el_toks)
            vk = b.op("scalar", lambda s_: s_.copy(out=qsc_bf, in_=psB),
                      deps=[q_rope, p1_done, prev_e2])
            krope_ps[0] = vk
            krope_ps[1] = vk
            ke_all = []
            prev_e2k = prev_e2
            for kh in range(2):
                pk = qsc_bf[:, kh * 512:(kh + 1) * 512].rearrange("p (h w) -> p h w", w=64)
                cc8 = cos[:, tt * 32:(tt + 1) * 32].unsqueeze(1).to_broadcast([128, 8, 32])
                ss8 = sin[:, tt * 32:(tt + 1) * 32].unsqueeze(1).to_broadcast([128, 8, 32])
                tv8 = t_scr.rearrange("p (h w) -> p h w", w=32)
                sv8 = sg_scr.rearrange("p (h w) -> p h w", w=32)
                k1 = b.op("vector", lambda v, pk=pk, cc8=cc8: v.tensor_tensor(
                    out=tv8[:, 0:8], in0=pk[:, :, 0:32], in1=cc8, op=ALU.mult),
                    deps=[vk, q_elu])
                k2 = b.op("vector", lambda v, pk=pk, ss8=ss8: v.tensor_tensor(
                    out=tv8[:, 8:16], in0=pk[:, :, 32:64], in1=ss8, op=ALU.mult),
                    deps=[vk])
                k3 = b.op("vector", lambda v, pk=pk, ss8=ss8: v.tensor_tensor(
                    out=sv8[:, 0:8], in0=pk[:, :, 0:32], in1=ss8, op=ALU.mult),
                    deps=[vk])
                k4 = b.op("vector", lambda v, pk=pk, cc8=cc8: v.tensor_tensor(
                    out=sv8[:, 8:16], in0=pk[:, :, 32:64], in1=cc8, op=ALU.mult),
                    deps=[vk])
                k5 = b.op("vector", lambda v: v.tensor_sub(
                    out=t_scr[:, 0:256], in0=t_scr[:, 0:256], in1=t_scr[:, 256:512]),
                    deps=[k1, k2])
                k6 = b.op("vector", lambda v: v.tensor_add(
                    out=sg_scr[:, 0:256], in0=sg_scr[:, 0:256], in1=sg_scr[:, 256:512]),
                    deps=[k3, k4])
                ktm = kf_tt[:, kh * 512:(kh + 1) * 512].rearrange("p (h w) -> p h w", w=64)
                for src, dst_sl in ((t_scr, 0), (sg_scr, 32)):
                    e1 = b.op("vector", lambda v, src=src: v.tensor_scalar(
                        out=u1_scr[:, 0:256], in0=src[:, 0:256], scalar1=0.0, scalar2=None,
                        op0=ALU.min), deps=[k5, k6, prev_e2k, kv_mm_tok[0]])
                    e2 = b.op("scalar", lambda s_: s_.activation(
                        out=u1_scr[:, 256:512], in_=u1_scr[:, 0:256], func=AF.Exp),
                        deps=[e1])
                    prev_e2k = e2
                    e4 = b.op("vector", lambda v, src=src, dst_sl=dst_sl, ktm=ktm:
                              v.scalar_tensor_tensor(
                                  out=ktm[:, :, dst_sl:dst_sl + 32],
                                  in0=src.rearrange("p (h w) -> p h w", w=32)[:, 0:8],
                                  scalar=0.0,
                                  in1=u1_scr[:, 256:512].rearrange(
                                      "p (h w) -> p h w", w=32)[:, 0:8],
                                  op0=ALU.max, op1=ALU.add), deps=[e2])
                    ke_all.append(e4)
            k_done = b._merge(ke_all)
            scr_tok[0] = b._merge([k_done, prev_e2k])
            pend[tt] = {"k_done": k_done, "vc": b._merge([vc0, vc1]),
                        "q_elu": q_elu, "vq": b._merge([vq, q_rope])}
        emit_tail(TT - 1)
        proj_done = b._merge([kv_mm_tok[0], tp_last[0], scr_tok[0], v1cp_last[0]])
        b.tap("qf_fm", qf_fm, [128, DT * T], proj_done, dtype=BF16)
        b.tap("kf_last", kf_tt, [128, H * HD], proj_done, dtype=BF16)
        b.tap("v_last", v_tt, [128, H * 65 + 8], proj_done, dtype=BF16)

        kv_ev = []
        for h in range(H):
            base = (h % 2) * 64
            tk = b.op("scalar", lambda s_, h=h, base=base: s_.copy(
                out=kvks[base:base + 64, (h // 2) * 66: (h // 2) * 66 + 65],
                in_=kv_ps(h)), deps=[kv_mm_tok[0]] + kv_ev[-1:])
            kv_ev.append(tk)
        kv_done = b._merge(kv_ev)
        b.tap("kvks", kvks, [128, 8 * 66], kv_done, dtype=BF16)

        out_toks = []
        NP = 3
        ops_slot = [psA[:, 0:512], psA[:, 512:1024], psD[:, 0:512]]
        rb_slot = [psB[:, 0:512], psB[:, 512:1024], psC[:, 0:512]]
        scr_slot = [t_scr, u1_scr, sg_scr]
        ps_free = {p: kv_done for p in range(NP)}
        rb_free = {p: kv_done for p in range(NP)}
        dn_scr = {p: b._merge([kv_done, scr_tok[0]]) for p in range(NP)}
        gidx = 0
        for h in range(H):
            odd = h % 2
            base = odd * 64
            kcol = (h // 2) * 66
            for ch in range(4):
                par = gidx % NP
                gidx += 1
                f32s = scr_slot[par]
                qsl = qf_fm[base:base + 64,
                            (h // 2) * T + ch * 512:(h // 2) * T + (ch + 1) * 512]
                ops_ = ops_slot[par]
                if not odd:
                    def omm(t, kcol=kcol, ops_=ops_, qsl=qsl):
                        return t.matmul(ops_[0:65], kvks[0:64, kcol:kcol + 65], qsl,
                                        start=True, stop=True)
                    ovp = ops_[0:64]
                else:
                    def omm(t, kcol=kcol, ops_=ops_, qsl=qsl):
                        t.matmul(ops_[64:128], kvks[64:128, kcol:kcol + 64], qsl,
                                 start=True, stop=True, tile_position=(64, 64))
                        return t.matmul(ops_[0:1], kvks[64:128, kcol + 64:kcol + 65],
                                        qsl, start=False, stop=True,
                                        tile_position=(64, 0))
                    ovp = ops_[64:128]
                tm_ = b.op("tensor", omm, deps=[kv_done, ps_free[par], tp_last[0]])
                dw = 65 if not odd else 32
                drow = 64 if not odd else 0
                d1 = b.op("vector", lambda v, ops_=ops_, dw=dw, f32s=f32s: v.tensor_scalar(
                    out=f32s[0:dw, 0:512], in0=ops_[0:dw], scalar1=1e-6, scalar2=None,
                    op0=ALU.max), deps=[tm_, dn_scr[par]])
                d1b = b.op("scalar", lambda s_, dw=dw, f32s=f32s: s_.activation(
                    out=f32s[0:dw, 0:512], in_=f32s[0:dw, 0:512], func=AF.Ln),
                    deps=[d1])
                d2 = b.op("scalar", lambda s_, dw=dw, f32s=f32s: s_.activation(
                    out=f32s[0:dw, 0:512], in_=f32s[0:dw, 0:512], func=AF.Exp,
                    scale=-1.0), deps=[d1b])
                rbp = rb_slot[par][64:128] if odd else rb_slot[par][0:64]
                tpz = (64, 0) if not odd else (0, 64)
                rb = b.op("tensor", lambda t, rbp=rbp, f32s=f32s, drow=drow, tpz=tpz:
                          t.matmul(rbp, onesf32[drow:drow + 1, 0:64],
                                   f32s[drow:drow + 1, 0:512],
                                   start=True, stop=True, tile_position=tpz),
                          deps=[d2, t_ones3, rb_free[par]])
                o16 = b.op("scalar", lambda s_, ovp=ovp, base=base, f32s=f32s: s_.copy(
                    out=f32s[base:base + 64, 512:1024], in_=ovp),
                    deps=[tm_, dn_scr[par]])
                ps_free[par] = b._merge([o16, d2])
                dst = out_all[base:base + 64,
                              (h // 2) * T + ch * 512:(h // 2) * T + (ch + 1) * 512]
                osc = b.op("vector", lambda v, dst=dst, rbp=rbp, base=base, f32s=f32s:
                           v.tensor_tensor(
                               out=dst, in0=f32s[base:base + 64, 512:1024], in1=rbp,
                               op=ALU.mult), deps=[rb, o16, proj_done])
                rb_free[par] = osc
                dn_scr[par] = osc
                out_toks.append(osc)
        out_done = b._merge(out_toks)

        x_toks = []
        op_free = {0: out_done, 1: out_done}
        for tt in range(TT):
            par = tt % 2
            pp = pairs[2 + par]
            mm_toks = []
            for dt in range(DT):
                sl = dt % 2
                two = b.dma("sync", at_wo[sl], woutT_blk[dt],
                            deps=[blk_toks[("wout", dt)], slotf[("wo", sl)]],
                            stream=f"wd{sl}")
                def pmm(t, tt=tt, dt=dt, sl=sl, pp=pp):
                    osl = out_all[:, dt * T + tt * 128: dt * T + tt * 128 + 128]
                    st, sp = (dt == 0), (dt == DT - 1)
                    t.matmul(pp[:, 0:512], osl, at_wo[sl][:, 0:512], start=st, stop=sp)
                    return t.matmul(pp[:, 512:1024], osl, at_wo[sl][:, 512:1024],
                                    start=st, stop=sp)
                dep = [two, out_done] + ([mm_toks[-2]] if len(mm_toks) >= 2
                                         else [op_free[par]])
                tok = b.op("tensor", pmm, deps=dep)
                mm_toks.append(tok)
                slotf[("wo", sl)] = tok
            tmm = mm_toks[-1]
            ev = []
            for n in range(2):
                xsl = x_sb[:, tt * D + n * 512: tt * D + n * 512 + 512]
                te = b.op("vector", lambda v, xsl=xsl, pp=pp, n=n: v.tensor_add(
                    out=xsl, in0=xsl, in1=pp[:, n * 512:(n + 1) * 512]), deps=[tmm])
                ev.append(te)
            op_free[par] = b._merge(ev)
            x_toks.extend(ev)
        x_done = b._merge(x_toks)
        gate = b._merge([x_done, proj_done, kv_done, out_done, op_free[0], op_free[1],
                         slotf[("wo", 0)], slotf[("wo", 1)]])
        return x_done, gate

    x_tok = [None]
    st_toks = []
    gate = [prologue_compute]
    arena = [None]
    for seq in range(SEQ):
        if seq == 0:
            t_x = early_x0
            for ee in range(min(4, n_experts)):
                emit_pro_cast(ee)
        else:
            t_x = b.dma("gpsimd", x_sb.rearrange("p (tt d) -> p tt d", d=D),
                        x_in[seq].rearrange("(tt p) d -> p tt d", p=128),
                        deps=[x_tok[0]], stream="xin")
        ln1 = emit_ln([t_x] + gate)
        b.tap(f"h1_s{seq}", h_fm, [128, DT * T], ln1, dtype=BF16)
        rt1 = emit_router([ln1, bias_done])
        if seq == 0:
            emit_rope_tables()
        b.tap(f"comb1_s{seq}", comb_sb, [128, TT * 8], rt1)
        done = rt1
        if seq == 0:
            emit_pro_expert(0)
            emit_pro_expert(1)

        def pf(e, pidx, seq=seq):
            if seq != 0:
                return
            if pidx < 0:
                if e + 4 < n_experts:
                    emit_pro_cast(e + 4)
                if e + 2 < n_experts:
                    queue_pro_expert(e + 2)
                if e == min(2, n_experts - 1) and do_attn:
                    emit_pro_attn_cast()
                if e == min(4, n_experts - 1) and do_attn:
                    emit_pro_attn()
                return
            pro_step(3)

        if do_ffn1:
            ffn1, g1 = emit_ffn([ln1, rt1], arena[0],
                                prefetch_cb=(pf if seq == 0 else None))
            arena[0] = g1
            gate = [g1]
            done = ffn1
            b.tap(f"xf1_s{seq}", x_sb, [128, TT * D], ffn1)
        if do_attn and ("wq", 0) not in blk_toks and not pro_queue:
            emit_pro_attn_cast()
            emit_pro_attn()
        while pro_queue:
            pro_queue.pop(0)()
        if do_attn:
            ln2 = emit_ln([done] + gate)
            at1, g2 = emit_attn([ln2], arena[0])
            arena[0] = g2
            gate = [g2]
            done = at1
            b.tap(f"xa_s{seq}", x_sb, [128, TT * D], at1)
        if do_ffn2:
            ln3 = emit_ln([done] + gate)
            rt2 = emit_router([ln3, bias_done])
            ffn2, g3 = emit_ffn([ln3, rt2], arena[0])
            arena[0] = g3
            gate = [g3]
            done = ffn2
        t_st = b.dma("sync", out[seq].rearrange("(tt p) d -> p tt d", p=128),
                     x_sb.rearrange("p (tt d) -> p tt d", d=D),
                     deps=[done, t_x] + st_toks[-1:], stream="xout")
        st_toks.append(t_st)
        x_tok[0] = t_st

    fin = b._merge(st_toks + [b.tok_all()])
    b.op("sync", lambda s: None, deps=[fin])

    b.finalize()
    return nc, b


_cache = {}


def kernel(**inputs):
    if "main" not in _cache:
        _cache["main"] = build_program()
    nc, b = _cache["main"]
    x = np.ascontiguousarray(inputs["L0_attn_x"])
    in_maps = []
    for c in range(CORES):
        m = {k: np.ascontiguousarray(v) for k, v in inputs.items() if k != "L0_attn_x"}
        m["L0_attn_x"] = x[c * SEQ:(c + 1) * SEQ]
        in_maps.append(m)
    res = run_bass_kernel_spmd(nc, in_maps, core_ids=list(range(CORES)))
    return np.concatenate([np.asarray(r["out"]) for r in res.results], axis=0)
